# version 17
# baseline (speedup 1.0000x reference)
import math
from contextlib import ExitStack

import numpy as np
import ml_dtypes

import concourse.bass as bass
import concourse.mybir as mybir
from concourse.bass_utils import run_bass_kernel_spmd

F32 = mybir.dt.float32
BF16 = mybir.dt.bfloat16
AF = mybir.ActivationFunctionType
ALU = mybir.AluOpType
AX = mybir.AxisListType

NCORES = 8
S_TOT = 8192
H = 2048
I_FF = 8192
TPC = 1024
NT = 8
HC = 16
NCH = 16
SIG = 1.0 / math.sqrt(2048.0)
EPS = 1e-5


def owned_tiles(r):
    return [4 * r + i for i in range(4)] + [60 - 4 * r + i for i in range(4)]


def chunk_rank_half(c):
    return (c, 0) if c <= 7 else (15 - c, 1)


def build_core():
    nc = bass.Bass("TRN2", target_bir_lowering=False)

    x_in = nc.dram_tensor("x", [TPC, H], F32, kind="ExternalInput").ap()
    qkt = nc.dram_tensor("qkt", [32 * 128, HC * 128], F32, kind="ExternalInput").ap()
    wvt = nc.dram_tensor("wvt", [4 * 128, HC * 512], F32, kind="ExternalInput").ap()
    wot = nc.dram_tensor("wot", [128, HC * H], F32, kind="ExternalInput").ap()
    wgt = nc.dram_tensor("wgt", [64 * 128, HC * 128], F32, kind="ExternalInput").ap()
    wut = nc.dram_tensor("wut", [64 * 128, HC * 128], F32, kind="ExternalInput").ap()
    wdt = nc.dram_tensor("wdt", [16 * 128, 64 * 128], F32, kind="ExternalInput").ap()
    gam2 = nc.dram_tensor("gam2", [128, HC], F32, kind="ExternalInput").ap()
    bet2 = nc.dram_tensor("bet2", [128, HC], F32, kind="ExternalInput").ap()
    idf_d = nc.dram_tensor("idf", [128, 128], F32, kind="ExternalInput").ap()
    idb_d = nc.dram_tensor("idb", [128, 128], BF16, kind="ExternalInput").ap()
    pmask_d = nc.dram_tensor("pmask", [NCH * NT * 128, 512], BF16, kind="ExternalInput").ap()
    out_d = nc.dram_tensor("out", [TPC, H], F32, kind="ExternalOutput").ap()

    k_loc = nc.dram_tensor("k_loc", [H, TPC], BF16, kind="Internal").ap()
    v_loc = nc.dram_tensor("v_loc", [TPC, H], BF16, kind="Internal").ap()
    q_dram = nc.dram_tensor("q_dram", [H, TPC], BF16, kind="Internal").ap()
    k_gath = nc.dram_tensor("k_gath", [NCORES * H, TPC], BF16, kind="Internal").ap()
    v_gath = nc.dram_tensor("v_gath", [NCORES * TPC, H], BF16, kind="Internal").ap()
    s_ffn = nc.dram_tensor("s_ffn", [H, TPC], BF16, kind="Internal").ap()

    ENGS = ("tensor", "vector", "scalar", "gpsimd", "sync")
    ops = {E: [] for E in ENGS}
    S = {"scst": 0, "sx0": 0, "sx1": 0,
         "swq0": 0, "swq1": 0, "swv0": 0, "swv1": 0,
         "skst0": 0, "skst1": 0, "svst0": 0, "svst1": 0, "sqst0": 0, "sqst1": 0,
         "skv0": 0, "skv1": 0, "spm0": 0, "spm1": 0,
         "sql": 0, "swo": 0,
         "swg0": 0, "swg1": 0, "swu0": 0, "swu1": 0,
         "swd0": 0, "swd1": 0, "sfst0": 0, "sfst1": 0,
         "sx70": 0, "sx71": 0, "sout0": 0, "sout1": 0,
         "scc": 0, "spe": 0, "sdv": 0, "sac": 0, "spl": 0}
    semh = {}
    pend = {E: [] for E in ENGS}

    OWN = {"vector": "sdv", "scalar": "sac"}

    def emit(E, waits, fn, inc=None, n=1):
        w = list(pend[E]) + [(nm, v) for nm, v in waits if v > 0]
        if E in OWN and S[OWN[E]] > 0:
            w.append((OWN[E], S[OWN[E]]))
        pend[E].clear()
        if inc is not None:
            S[inc] += n
        def run(e, w=tuple(w), fn=fn, inc=inc, n=n):
            for nm, v in w:
                e.wait_ge(semh[nm], v)
            rr = fn(e)
            if inc is not None:
                rr.then_inc(semh[inc], n)
        ops[E].append(run)
        return S[inc] if inc is not None else None

    def barrier():
        snap = [(nm, v) for nm, v in S.items() if v > 0 and nm != "scc"]
        for E in ENGS:
            pend[E] = list(snap)

    stacks = {}

    def open_group(name):
        stacks[name] = ExitStack()
        return stacks[name]

    def close_group(name):
        stacks[name].close()
        del stacks[name]

    outer = ExitStack()
    with outer:
        e_ = outer.enter_context
        idf = e_(nc.sbuf_tensor("idf_s", [128, 128], F32))
        idb = e_(nc.sbuf_tensor("idb_s", [128, 128], BF16))
        gam = e_(nc.sbuf_tensor("gam_s", [128, HC], F32))
        bet = e_(nc.sbuf_tensor("bet_s", [128, HC], F32))
        epsc = e_(nc.sbuf_tensor("eps_s", [128, 1], F32))
        mu_a = e_(nc.sbuf_tensor("mu_a", [128, NT], F32))
        rstd_a = e_(nc.sbuf_tensor("rstd_a", [128, NT], F32))
        a_a = e_(nc.sbuf_tensor("a_a", [128, NT], F32))
        b_a = e_(nc.sbuf_tensor("b_a", [128, NT], F32))
        sm_m = e_(nc.sbuf_tensor("sm_m", [128, 2 * NT], F32))
        sm_l = e_(nc.sbuf_tensor("sm_l", [128, NT], F32))
        sm_al = e_(nc.sbuf_tensor("sm_al", [128, NT], F32))
        sm_d = e_(nc.sbuf_tensor("sm_d", [128, NT], F32))
        sm_sc = e_(nc.sbuf_tensor("sm_sc", [128, NT], F32))
        sm_t1 = e_(nc.sbuf_tensor("sm_t1", [128, NT], F32))

        emit("sync", [], lambda e: e.dma_start(idf[:], idf_d[:]), "scst", 16)
        emit("sync", [], lambda e: e.dma_start(idb[:], idb_d[:]), "scst", 16)
        emit("sync", [], lambda e: e.dma_start(gam[:], gam2[:]), "scst", 16)
        emit("sync", [], lambda e: e.dma_start(bet[:], bet2[:]), "scst", 16)
        consts_done = S["scst"]
        emit("vector", [], lambda e: e.memset(epsc[:], EPS), "sdv")


        gXN = open_group("XN")
        xn_fm = gXN.enter_context(nc.sbuf_tensor("xn_fm", [128, HC * TPC], BF16))
        gA1 = open_group("A1")
        xnorm = gA1.enter_context(nc.sbuf_tensor("xnorm", [128, HC * TPC], BF16))
        g2 = open_group("P2")
        wqk = [g2.enter_context(nc.sbuf_tensor(f"wqk{i}", [128, HC * 128], BF16)) for i in range(2)]
        wvb = [g2.enter_context(nc.sbuf_tensor(f"wvb{i}", [128, HC * 512], BF16)) for i in range(2)]
        qkev = [g2.enter_context(nc.sbuf_tensor(f"qkev{i}", [128, TPC], BF16)) for i in range(2)]
        vev = [g2.enter_context(nc.sbuf_tensor(f"vev{i}", [128, 512], BF16)) for i in range(2)]
        ps2 = open_group("PS2")
        ps_qk = [ps2.enter_context(nc.psum_tensor(f"ps_qk{i}", [128, 1024], F32)) for i in range(2)]
        ps_v = [ps2.enter_context(nc.psum_tensor(f"ps_v{i}", [128, 512], F32)) for i in range(2)]
        g1 = open_group("P1")
        xt = [g1.enter_context(nc.sbuf_tensor(f"xt{i}", [128, H], F32)) for i in range(2)]
        xct = [g1.enter_context(nc.sbuf_tensor(f"xct{i}", [128, H], F32)) for i in range(2)]
        xnt = [g1.enter_context(nc.sbuf_tensor(f"xnt{i}", [128, H], F32)) for i in range(2)]
        xc2t = xct
        yt = [g1.enter_context(nc.sbuf_tensor(f"yt{i}", [128, H], F32)) for i in range(2)]
        sqt = g1.enter_context(nc.sbuf_tensor("sqt", [128, H], F32))
        vr = g1.enter_context(nc.sbuf_tensor("vr", [128, 4], F32))
        ps1 = open_group("PS1")
        ps_tr = [ps1.enter_context(nc.psum_tensor(f"ps_tr{i}", [128, 128], F32)) for i in range(2)]

        x_loaded = [0] * NT
        dv_tile = [0] * NT
        pe_trdone = [0] * NT
        ev_done = [[("sdv", 0)] * 32 for _ in range(NT)]

        for tl in range(NT):
            b2 = tl % 2
            w = [("scst", consts_done)]
            if tl >= 2:
                w.append(("sdv", dv_tile[tl - 2]))
            x_loaded[tl] = emit("sync", w,
                lambda e, tl=tl, b2=b2: e.dma_start(xt[b2][:], x_in[tl * 128:(tl + 1) * 128, :]),
                f"sx{b2}", 16)
            w = [(f"sx{b2}", x_loaded[tl])]
            if tl >= 2:
                w += [ev_done[tl - 2][31], ev_done[tl - 2][15], ("spe", pe_trdone[tl - 2])]
            emit("vector", w,
                 lambda e, b2=b2: e.reduce_sum(vr[:, 0:1], xt[b2][:], axis=AX.X), "sdv")
            emit("vector", [],
                 lambda e, tl=tl: e.tensor_scalar_mul(mu_a[:, tl:tl + 1], vr[:, 0:1], 1.0 / H), "sdv")
            emit("vector", [],
                 lambda e, b2=b2, tl=tl: e.tensor_scalar(xct[b2][:], xt[b2][:], mu_a[:, tl:tl + 1], None, ALU.subtract),
                 "sdv")
            ac_sq = emit("scalar", [("sdv", S["sdv"])],
                 lambda e, b2=b2: e.activation(sqt[:], xct[b2][:], AF.Square, accum_out=vr[:, 1:2]), "sac")
            emit("vector", [("sac", ac_sq)],
                 lambda e: e.tensor_scalar_mul(vr[:, 1:2], vr[:, 1:2], 1.0 / H), "sdv")
            emit("scalar", [("sdv", S["sdv"])],
                 lambda e: e.activation(vr[:, 2:3], vr[:, 1:2], AF.Ln, bias=epsc[:, 0:1]), "sac")
            emit("scalar", [],
                 lambda e, tl=tl: e.activation(rstd_a[:, tl:tl + 1], vr[:, 2:3], AF.Exp, scale=-0.5), "sac")
            dv_xn = emit("vector", [("sac", S["sac"])],
                 lambda e, b2=b2, tl=tl: e.tensor_scalar(xnt[b2][:], xct[b2][:], rstd_a[:, tl:tl + 1], None, ALU.mult),
                 "sdv")
            emit("vector", [],
                 lambda e, b2=b2: e.reduce_sum(vr[:, 3:4], xnt[b2][:], axis=AX.X), "sdv")
            emit("vector", [],
                 lambda e: e.tensor_scalar_mul(vr[:, 3:4], vr[:, 3:4], 1.0 / H), "sdv")
            emit("vector", [],
                 lambda e, b2=b2: e.tensor_scalar(xc2t[b2][:], xnt[b2][:], vr[:, 3:4], None, ALU.subtract), "sdv")
            ac_sq2 = emit("scalar", [("sdv", S["sdv"])],
                 lambda e, b2=b2: e.activation(sqt[:], xc2t[b2][:], AF.Square, accum_out=vr[:, 1:2]), "sac")
            emit("vector", [("sac", ac_sq2)],
                 lambda e: e.tensor_scalar_mul(vr[:, 1:2], vr[:, 1:2], 1.0 / H), "sdv")
            emit("scalar", [("sdv", S["sdv"])],
                 lambda e: e.activation(vr[:, 2:3], vr[:, 1:2], AF.Ln, bias=epsc[:, 0:1]), "sac")
            ac_r2 = emit("scalar", [],
                 lambda e: e.activation(vr[:, 3:4], vr[:, 2:3], AF.Exp, scale=-0.5), "sac")
            dv_y = emit("vector", [("sac", ac_r2)],
                 lambda e, b2=b2: e.tensor_scalar(yt[b2][:], xc2t[b2][:], vr[:, 3:4], None, ALU.mult), "sdv")
            dv_tile[tl] = dv_y
            for idx in range(32):
                is_y = idx >= 16
                fc = idx % 16
                slot = idx % 2
                src = yt[b2] if is_y else xnt[b2]
                w = [("sdv", dv_y if is_y else dv_xn)]
                if idx >= 2:
                    w.append(ev_done[tl][idx - 2])
                elif tl >= 2:
                    w.append(ev_done[tl - 2][30 + idx])
                pe_v = emit("tensor", w,
                    lambda e, src=src, fc=fc, slot=slot:
                        e.transpose(ps_tr[slot][:], src[:, fc * 128:(fc + 1) * 128], idf[:]),
                    "spe")
                if is_y:
                    def ev(e, fc=fc, slot=slot, tl=tl):
                        return e.tensor_scalar(
                            xnorm[:, fc * TPC + tl * 128: fc * TPC + (tl + 1) * 128],
                            ps_tr[slot][:],
                            gam[:, fc:fc + 1], bet[:, fc:fc + 1], ALU.mult, ALU.add)
                    ev_done[tl][idx] = ("sdv", emit("vector", [("spe", pe_v)], ev, "sdv"))
                else:
                    def ev(e, fc=fc, slot=slot, tl=tl):
                        return e.copy(xn_fm[:, fc * TPC + tl * 128: fc * TPC + (tl + 1) * 128],
                                      ps_tr[slot][:])
                    ev_done[tl][idx] = ("sac", emit("scalar", [("spe", pe_v)], ev, "sac"))
            pe_trdone[tl] = S["spe"]

        emit("vector", [("sac", S["sac"])],
             lambda e: e.tensor_scalar(a_a[:], rstd_a[:], 1.0, None, ALU.add), "sdv")
        emit("vector", [],
             lambda e: e.tensor_tensor(b_a[:], mu_a[:], rstd_a[:], ALU.mult), "sdv")
        emit("vector", [],
             lambda e: e.tensor_scalar_mul(b_a[:], b_a[:], -1.0), "sdv")

        close_group("PS1")
        close_group("P1")

        pe_vp = [0] * 32
        ac_vp = [0] * 32
        st_v = [0] * 32
        for cb in range(4):
            b2 = cb % 2
            w = []
            if cb >= 2:
                w.append(("spe", pe_vp[(cb - 2) * NT + NT - 1]))
            wldv = emit("gpsimd", w,
                lambda e, b2=b2, cb=cb: e.dma_start(
                    wvb[b2][:], wvt[cb * 128:(cb + 1) * 128, :]),
                f"swv{b2}", 16)
            for tl in range(NT):
                vi = cb * NT + tl
                pb = vi % 2
                for hc in range(HC):
                    w = []
                    if hc == 0:
                        w.append((f"swv{b2}", wldv))
                        w.append(ev_done[tl][31])
                        if vi >= 2:
                            w.append(("sac", ac_vp[vi - 2]))
                    pe_vp[vi] = emit("tensor", w,
                        lambda e, pb=pb, b2=b2, hc=hc, tl=tl: e.matmul(
                            ps_v[pb][:],
                            xnorm[:, hc * TPC + tl * 128: hc * TPC + (tl + 1) * 128],
                            wvb[b2][:, hc * 512:(hc + 1) * 512],
                            start=(hc == 0), stop=(hc == HC - 1)),
                        "spe" if hc == HC - 1 else None)
                w = [("spe", pe_vp[vi])]
                if vi >= 2:
                    w.append((f"svst{pb}", st_v[vi - 2]))
                ac_vp[vi] = emit("scalar", w,
                    lambda e, pb=pb: e.copy(vev[pb][:], ps_v[pb][:]), "sac")
                st_v[vi] = emit("sync", [("sac", ac_vp[vi])],
                    lambda e, pb=pb, tl=tl, cb=cb: e.dma_start(
                        v_loc[tl * 128:(tl + 1) * 128, cb * 512:(cb + 1) * 512], vev[pb][:]),
                    f"svst{pb}", 16)

        emit("gpsimd", [("svst0", S["svst0"]), ("svst1", S["svst1"])],
            lambda e: e.collective_compute("AllGather", ALU.bypass,
                replica_groups=[list(range(NCORES))],
                ins=[v_loc.opt()], outs=[v_gath.opt()]), "scc", 1)

        pe_qk = [0] * 32
        ac_qk = [0] * 32
        st_q = [0] * 16
        st_k = [0] * 16
        for m in range(32):
            is_q = m < 16
            fc = m % 16
            b2 = m % 2
            w = []
            if m >= 2:
                w.append(("spe", pe_qk[m - 2]))
            wld = emit("gpsimd", w,
                lambda e, b2=b2, m=m: e.dma_start(
                    wqk[b2][:], qkt[m * 128:(m + 1) * 128, :]),
                f"swq{b2}", 16)
            for hc in range(HC):
                for n in range(2):
                    w = []
                    if hc == 0 and n == 0:
                        w.append((f"swq{b2}", wld))
                        if m == 0:
                            w.append(ev_done[NT - 1][31])
                        if m >= 2:
                            w.append(("sac", ac_qk[m - 2]))
                    last = (hc == HC - 1 and n == 1)
                    pe_qk[m] = emit("tensor", w,
                        lambda e, b2=b2, hc=hc, n=n: e.matmul(
                            ps_qk[b2][:, n * 512:(n + 1) * 512],
                            wqk[b2][:, hc * 128:(hc + 1) * 128],
                            xnorm[:, hc * TPC + n * 512: hc * TPC + (n + 1) * 512],
                            start=(hc == 0), stop=(hc == HC - 1)),
                        "spe" if last else None)
            sem = f"sqst{b2}" if is_q else f"skst{b2}"
            starr = st_q if is_q else st_k
            w = [("spe", pe_qk[m])]
            if fc >= 2:
                w.append((sem, starr[fc - 2]))
            ac_qk[m] = emit("scalar", w,
                lambda e, b2=b2: e.copy(qkev[b2][:], ps_qk[b2][:]), "sac")
            dst = q_dram if is_q else k_loc
            starr[fc] = emit("sync", [("sac", ac_qk[m])],
                lambda e, b2=b2, fc=fc, dst=dst: e.dma_start(dst[fc * 128:(fc + 1) * 128, :], qkev[b2][:]),
                sem, 16)

        emit("gpsimd", [("skst0", S["skst0"]), ("skst1", S["skst1"])],
            lambda e: e.collective_compute("AllGather", ALU.bypass,
                replica_groups=[list(range(NCORES))],
                ins=[k_loc.opt()], outs=[k_gath.opt()]), "scc", 1)

        barrier()
        close_group("PS2")
        close_group("P2")
        close_group("A1")

        g5o = open_group("HS")
        h_s = g5o.enter_context(nc.sbuf_tensor("h_s", [128, 64 * TPC], BF16))
        g5 = open_group("P5")
        wgb = [g5.enter_context(nc.sbuf_tensor(f"wgb{i}", [128, HC * 128], BF16)) for i in range(2)]
        wub = [g5.enter_context(nc.sbuf_tensor(f"wub{i}", [128, HC * 128], BF16)) for i in range(2)]
        gsb = [g5.enter_context(nc.sbuf_tensor(f"gsb{i}", [128, 512], F32)) for i in range(2)]
        hjt = [g5.enter_context(nc.sbuf_tensor(f"hjt{j}", [128, TPC], F32)) for j in range(4)]
        sjt = [g5.enter_context(nc.sbuf_tensor(f"sjt{j}", [128, 512], F32)) for j in range(4)]
        aux = [g5.enter_context(nc.sbuf_tensor(f"aux{j}", [128, 512], F32)) for j in range(1)]
        ps5 = open_group("PS5")
        ps_g = [ps5.enter_context(nc.psum_tensor(f"ps_g{i}", [128, 1024], F32)) for i in range(2)]
        ps_u = [ps5.enter_context(nc.psum_tensor(f"ps_u{i}", [128, 1024], F32)) for i in range(2)]

        pe_gu = {}
        dv_h = {}
        ac_silu = {}
        last_dvh = 0
        prev_h = {0: None, 1: None}
        for b in range(16):
            for j in range(4):
                ic = 16 * j + b
                q = b * 4 + j
                pp = q % 2
                w = []
                if q >= 2:
                    w.append(("spe", pe_gu[q - 2]))
                wldg = emit("gpsimd", w,
                    lambda e, pp=pp, ic=ic: e.dma_start(
                        wgb[pp][:], wgt[ic * 128:(ic + 1) * 128, :]),
                    f"swg{pp}", 16)
                wld = emit("gpsimd", [],
                    lambda e, pp=pp, ic=ic: e.dma_start(
                        wub[pp][:], wut[ic * 128:(ic + 1) * 128, :]),
                    f"swu{pp}", 16)
                for which in range(2):
                    wsrc = wgb[pp] if which == 0 else wub[pp]
                    pdst = ps_g[pp] if which == 0 else ps_u[pp]
                    for hc in range(HC):
                        for n in range(2):
                            w = []
                            if which == 0 and hc == 0 and n == 0:
                                w.append((f"swg{pp}", wldg))
                                w.append((f"swu{pp}", wld))
                                if q >= 2:
                                    w.append(("sdv", dv_h[(q - 2, 1)]))
                            last = (which == 1 and hc == HC - 1 and n == 1)
                            pe_gu[q] = emit("tensor", w,
                                lambda e, wsrc=wsrc, pdst=pdst, hc=hc, n=n: e.matmul(
                                    pdst[:, n * 512:(n + 1) * 512],
                                    wsrc[:, hc * 128:(hc + 1) * 128],
                                    xn_fm[:, hc * TPC + n * 512: hc * TPC + (n + 1) * 512],
                                    start=(hc == 0), stop=(hc == HC - 1)),
                                "spe" if last else None)
                for th in range(2):
                    w = [("spe", pe_gu[q])]
                    if prev_h[th] is not None:
                        w.append(("sdv", prev_h[th]))
                    ac_silu[(q, th)] = emit("scalar", w,
                        lambda e, pp=pp, th=th: e.activation(
                            gsb[th][:], ps_g[pp][:, th * 512:(th + 1) * 512], AF.Silu), "sac")
                    dv_h[(q, th)] = emit("vector", [("sac", ac_silu[(q, th)])],
                        lambda e, pp=pp, th=th, j=j: e.tensor_tensor(
                            hjt[j][:, th * 512:(th + 1) * 512], gsb[th][:],
                            ps_u[pp][:, th * 512:(th + 1) * 512], ALU.mult), "sdv")
                    prev_h[th] = dv_h[(q, th)]
                    last_dvh = dv_h[(q, th)]
                    if j == 3:
                        sl = slice(th * 512, (th + 1) * 512)
                        for jj in range(4):
                            emit("vector", [],
                                 lambda e, jj=jj, sl=sl: e.tensor_tensor(
                                     sjt[jj][:], hjt[jj][:, sl], hjt[jj][:, sl], ALU.mult), "sdv")
                        emit("vector", [], lambda e: e.tensor_tensor(aux[0][:], sjt[0][:], sjt[1][:], ALU.max), "sdv")
                        emit("vector", [], lambda e: e.tensor_tensor(sjt[0][:], sjt[0][:], sjt[1][:], ALU.min), "sdv")
                        emit("vector", [], lambda e: e.tensor_tensor(sjt[1][:], sjt[2][:], sjt[3][:], ALU.max), "sdv")
                        emit("vector", [], lambda e: e.tensor_tensor(sjt[2][:], sjt[2][:], sjt[3][:], ALU.min), "sdv")
                        emit("vector", [], lambda e: e.tensor_tensor(aux[0][:], aux[0][:], sjt[1][:], ALU.min), "sdv")
                        emit("vector", [], lambda e: e.tensor_tensor(sjt[0][:], sjt[0][:], sjt[2][:], ALU.max), "sdv")
                        emit("vector", [], lambda e: e.tensor_tensor(aux[0][:], aux[0][:], sjt[0][:], ALU.max), "sdv")
                        for jj in range(4):
                            ic2 = 16 * jj + b
                            emit("vector", [],
                                 lambda e, jj=jj, sl=sl: e.tensor_tensor(
                                     sjt[1][:], hjt[jj][:, sl], hjt[jj][:, sl], ALU.mult), "sdv")
                            emit("vector", [],
                                 lambda e: e.tensor_tensor(sjt[1][:], sjt[1][:], aux[0][:], ALU.is_ge), "sdv")
                            last_dvh = emit("vector", [],
                                 lambda e, jj=jj, sl=sl, ic2=ic2, th=th: e.tensor_tensor(
                                     h_s[:, ic2 * TPC + th * 512: ic2 * TPC + (th + 1) * 512],
                                     hjt[jj][:, sl], sjt[1][:], ALU.mult), "sdv")

        barrier()
        close_group("PS5")
        close_group("P5")

        g6 = open_group("P6")
        wdb = [g6.enter_context(nc.sbuf_tensor(f"wdb{i}", [128, 64 * 128], BF16)) for i in range(2)]
        sfev = [g6.enter_context(nc.sbuf_tensor(f"sfev{i}", [128, TPC], BF16)) for i in range(2)]
        ps6 = open_group("PS6")
        ps_f = [ps6.enter_context(nc.psum_tensor(f"ps_f{i}", [128, 1024], F32)) for i in range(2)]
        pe_f = [0] * 16
        ac_f = [0] * 16
        st_f = [0] * 16
        for fc in range(16):
            pb = fc % 2
            w = []
            if fc >= 2:
                w.append(("spe", pe_f[fc - 2]))
            wldd = emit("gpsimd", w,
                lambda e, pb=pb, fc=fc: e.dma_start(
                    wdb[pb][:], wdt[fc * 128:(fc + 1) * 128, :]),
                f"swd{pb}", 16)
            for ic in range(64):
                for n in range(2):
                    w = []
                    if ic == 0 and n == 0:
                        w.append((f"swd{pb}", wldd))
                        if fc >= 2:
                            w.append(("sac", ac_f[fc - 2]))
                    last = (ic == 63 and n == 1)
                    pe_f[fc] = emit("tensor", w,
                        lambda e, pb=pb, ic=ic, n=n: e.matmul(
                            ps_f[pb][:, n * 512:(n + 1) * 512],
                            wdb[pb][:, ic * 128:(ic + 1) * 128],
                            h_s[:, ic * TPC + n * 512: ic * TPC + (n + 1) * 512],
                            start=(ic == 0), stop=(ic == 63)),
                        "spe" if last else None)
            w = [("spe", pe_f[fc])]
            if fc >= 2:
                w.append((f"sfst{pb}", st_f[fc - 2]))
            ac_f[fc] = emit("scalar", w,
                lambda e, pb=pb: e.copy(sfev[pb][:], ps_f[pb][:]), "sac")
            st_f[fc] = emit("sync", [("sac", ac_f[fc])],
                lambda e, pb=pb, fc=fc: e.dma_start(
                    s_ffn[fc * 128:(fc + 1) * 128, :], sfev[pb][:]), f"sfst{pb}", 16)

        barrier()
        close_group("PS6")
        close_group("P6")
        close_group("HS")
        close_group("XN")

        gacc = open_group("ACC")
        acc = gacc.enter_context(nc.sbuf_tensor("acc", [128, NT * H], F32))
        g3 = open_group("P3")
        q_fm = g3.enter_context(nc.sbuf_tensor("q_fm", [128, HC * TPC], BF16))
        kbuf = [g3.enter_context(nc.sbuf_tensor(f"kbuf{i}", [128, HC * 512], BF16)) for i in range(2)]
        vbuf = [g3.enter_context(nc.sbuf_tensor(f"vbuf{i}", [128, 4 * H], BF16)) for i in range(2)]
        p_sb = [g3.enter_context(nc.sbuf_tensor(f"p_sb{i}", [128, 512], BF16)) for i in range(2)]
        pt_sb = [g3.enter_context(nc.sbuf_tensor(f"pt_sb{i}", [128, 512], BF16)) for i in range(2)]
        pm_sb = [g3.enter_context(nc.sbuf_tensor(f"pm_sb{i}", [128, NT * 512], BF16)) for i in range(2)]
        ps3 = open_group("PS3")
        ps_s = [ps3.enter_context(nc.psum_tensor(f"ps_s{i}", [128, 512], F32)) for i in range(2)]
        ps_pt = [ps3.enter_context(nc.psum_tensor(f"ps_pt{i}", [128, 512], BF16)) for i in range(2)]
        ps_pv = ps3.enter_context(nc.psum_tensor("ps_pv", [128, H], F32))

        q_ld = emit("sync", [],
            lambda e: e.dma_start(q_fm.rearrange("p (fc t) -> p fc t", fc=HC),
                                  q_dram.rearrange("(fc p) t -> p fc t", p=128)), "sql", 16)

        NP = NCH * NT
        kv_ld = [0] * NCH
        pm_ld = [0] * NCH
        pe_sc = [0] * NP
        pe_tr3 = [0] * NP
        pe_pv3 = [0] * NP
        dv_pmul = [0] * NP
        dv_rsum = [0] * NP
        dv_state = [0] * NP
        dv_ptev = [0] * NP
        dv_accu = [0] * NP
        dv_lup = [0] * NP
        ac_al = [0] * NP
        ac_ex = [0] * NP
        pe_chunk_end = [0] * NCH

        def emit_kv_load(c):
            rr, hh = chunk_rank_half(c)
            w = [("scc", 1)]
            if c >= 2:
                w.append(("spe", pe_chunk_end[c - 2]))
            emit("sync", w,
                lambda e, rr=rr, hh=hh, c=c: e.dma_start(
                    vbuf[c % 2].rearrange("p (kb f) -> p kb f", kb=4),
                    v_gath[rr * TPC + hh * 512: rr * TPC + (hh + 1) * 512, :]
                    .rearrange("(kb p) f -> p kb f", p=128)), f"skv{c % 2}", 16)
            kv_ld[c] = emit("sync", [("scc", 2)],
                lambda e, rr=rr, hh=hh, c=c: e.dma_start(
                    kbuf[c % 2].rearrange("p (fc k) -> p fc k", fc=HC),
                    k_gath[rr * H:(rr + 1) * H, hh * 512:(hh + 1) * 512]
                    .rearrange("(fc p) k -> p fc k", p=128)), f"skv{c % 2}", 16)

        def emit_pm_load(c):
            w = []
            if c >= 2:
                w.append(("spe", pe_chunk_end[c - 2]))
            pm_ld[c] = emit("scalar", w,
                lambda e, c=c: e.dma_start(
                    pm_sb[c % 2].rearrange("p (tl k) -> p tl k", tl=NT),
                    pmask_d[c * NT * 128:(c + 1) * NT * 128, :]
                    .rearrange("(tl p) k -> p tl k", p=128)),
                f"spm{c % 2}", 16)

        def emit_scores(i):
            c, tl = divmod(i, NT)
            sb = i % 2
            for fc in range(HC):
                w = []
                if fc == 0:
                    w.append((f"skv{c % 2}", kv_ld[c]))
                    w.append(("sql", q_ld))
                    if i >= 2:
                        w.append(("sac", ac_ex[i - 2]))
                emit("tensor", w,
                    lambda e, sb=sb, fc=fc, tl=tl, c=c: e.matmul(
                        ps_s[sb][:, :],
                        q_fm[:, fc * TPC + tl * 128: fc * TPC + (tl + 1) * 128],
                        kbuf[c % 2][:, fc * 512:(fc + 1) * 512],
                        start=(fc == 0), stop=False),
                    None)
            pe_sc[i] = emit("tensor", [(f"spm{c % 2}", pm_ld[c])],
                lambda e, sb=sb, c=c, tl=tl: e.matmul(
                    ps_s[sb][:, :], idb[:],
                    pm_sb[c % 2][:, tl * 512:(tl + 1) * 512],
                    start=False, stop=True),
                "spe")

        def emit_softmax(i):
            c, tl = divmod(i, NT)
            sb = i % 2
            first = (c == 0)
            mo = (c % 2) * NT + tl
            mp = ((c - 1) % 2) * NT + tl
            emit("vector", [("spe", pe_sc[i])],
                 lambda e, sb=sb: e.reduce_max(sm_t1[:, 0:1], ps_s[sb][:, :],
                                               axis=AX.X, negate=True), "sdv")
            if first:
                dv_state[i] = emit("vector", [],
                    lambda e, mo=mo: e.tensor_scalar_mul(sm_m[:, mo:mo + 1], sm_t1[:, 0:1], SIG), "sdv")
            else:
                dv_state[i] = emit("vector", [],
                    lambda e, mo=mo, mp=mp: e.tensor_scalar(
                        sm_m[:, mo:mo + 1], sm_t1[:, 0:1], SIG, sm_m[:, mp:mp + 1],
                        ALU.mult, ALU.min), "sdv")
            w = [("sdv", dv_state[i])]
            if i >= 2:
                w.append(("spe", pe_tr3[i - 2]))
            ac_ex[i] = emit("scalar", w,
                lambda e, sb=sb, pb=i % 2, mo=mo, tl=tl: e.activation(
                    p_sb[pb][:], ps_s[sb][:], AF.Exp,
                    bias=sm_m[:, mo:mo + 1], scale=SIG,
                    accum_out=sm_sc[:, tl:tl + 1]), "sac")
            dv_pmul[i] = 0
            dv_rsum[i] = 0

        def emit_alpha(i):
            c, tl = divmod(i, NT)
            if c == 0:
                return
            mo = (c % 2) * NT + tl
            mp = ((c - 1) % 2) * NT + tl
            dv_d = emit("vector", [],
                lambda e, mo=mo, mp=mp, tl=tl: e.tensor_tensor(
                    sm_d[:, tl:tl + 1], sm_m[:, mo:mo + 1], sm_m[:, mp:mp + 1], ALU.subtract), "sdv")
            ac_al[i] = emit("scalar", [("sdv", dv_d)],
                lambda e, tl=tl: e.activation(sm_al[:, tl:tl + 1], sm_d[:, tl:tl + 1], AF.Exp), "sac")

        def emit_tr(i):
            c, tl = divmod(i, NT)
            pb = i % 2
            for kb in range(4):
                w = []
                if kb == 0:
                    w.append(("sac", ac_ex[i]))
                    if i >= 2:
                        w.append(("sac", dv_ptev[i - 2]))
                pe_tr3[i] = emit("tensor", w,
                    lambda e, pb=pb, kb=kb: e.transpose(
                        ps_pt[pb][:, kb * 128:(kb + 1) * 128],
                        p_sb[pb][:, kb * 128:(kb + 1) * 128], idb[:]),
                    "spe" if kb == 3 else None)
            w = [("spe", pe_tr3[i])]
            if i >= 2:
                w.append(("spe", pe_pv3[i - 2]))
            dv_ptev[i] = emit("scalar", w,
                lambda e, pb=pb: e.copy(pt_sb[pb][:], ps_pt[pb][:]), "sac")

        def emit_pv(i):
            c, tl = divmod(i, NT)
            sb = i % 2
            first = (c == 0)
            for kb in range(4):
                for n in range(4):
                    w = []
                    if kb == 0 and n == 0:
                        w.append(("sac", dv_ptev[i]))
                        if i >= 1:
                            w.append(("sdv", dv_accu[i - 1]))
                    last = (kb == 3 and n == 3)
                    pe_pv3[i] = emit("tensor", w,
                        lambda e, sb=sb, kb=kb, n=n, c=c: e.matmul(
                            ps_pv[:, n * 512:(n + 1) * 512],
                            pt_sb[sb][:, kb * 128:(kb + 1) * 128],
                            vbuf[c % 2][:, kb * H + n * 512: kb * H + (n + 1) * 512],
                            start=(kb == 0), stop=(kb == 3)),
                        "spe" if last else None)
            w = [("spe", pe_pv3[i])]
            if not first:
                w.append(("sac", ac_al[i]))
            if first:
                dv_accu[i] = emit("vector", w,
                    lambda e, tl=tl: e.tensor_copy(acc[:, tl * H:(tl + 1) * H], ps_pv[:]), "sdv")
                dv_lup[i] = emit("vector", [("sac", ac_ex[i])],
                    lambda e, tl=tl: e.tensor_copy(sm_l[:, tl:tl + 1], sm_sc[:, tl:tl + 1]), "sdv")
            else:
                dv_accu[i] = emit("vector", w,
                    lambda e, tl=tl: e.scalar_tensor_tensor(
                        acc[:, tl * H:(tl + 1) * H], acc[:, tl * H:(tl + 1) * H],
                        sm_al[:, tl:tl + 1], ps_pv[:], ALU.mult, ALU.add), "sdv")
                dv_lup[i] = emit("vector", [("sac", ac_ex[i])],
                    lambda e, tl=tl: e.scalar_tensor_tensor(
                        sm_l[:, tl:tl + 1], sm_l[:, tl:tl + 1],
                        sm_al[:, tl:tl + 1], sm_sc[:, tl:tl + 1], ALU.mult, ALU.add), "sdv")

        emit_kv_load(0)
        emit_pm_load(0)
        emit_kv_load(1)
        emit_pm_load(1)
        emit_scores(0)
        emit_softmax(0)
        emit_scores(1)
        for i in range(NP):
            ci4, tl4 = divmod(i + 5, NT)
            if i + 5 < NP and tl4 == 0 and ci4 >= 2:
                pe_chunk_end[ci4 - 2] = S["spe"]
                emit_kv_load(ci4)
                emit_pm_load(ci4)
            if i + 1 < NP:
                emit_softmax(i + 1)
            emit_tr(i)
            if i + 1 < NP:
                emit_alpha(i + 1)
            if i + 2 < NP:
                emit_scores(i + 2)
            emit_pv(i)
        pe_chunk_end[NCH - 1] = S["spe"]

        barrier()
        close_group("PS3")
        close_group("P3")

        gWO = open_group("WO")
        wo_sb = gWO.enter_context(nc.sbuf_tensor("wo_sb", [128, HC * H], BF16))
        anout = gWO.enter_context(nc.sbuf_tensor("anout", [128, HC * TPC], BF16))
        g4a = open_group("P4A")
        antm = [g4a.enter_context(nc.sbuf_tensor(f"antm{i}", [128, H], BF16)) for i in range(2)]
        ps4a = open_group("PS4A")
        ps_t4 = [ps4a.enter_context(nc.psum_tensor(f"ps_t4{i}", [128, 128], BF16)) for i in range(2)]

        wo_ld = emit("gpsimd", [],
            lambda e: e.dma_start(wo_sb[:], wot[:]), "swo", 16)

        ac_ev4 = [[0] * 16 for _ in range(NT)]
        pe_tr4 = [0] * NT
        for tl in range(NT):
            b2 = tl % 2
            emit("vector", [],
                 lambda e, tl=tl: e.reciprocal(sm_t1[:, 0:1], sm_l[:, tl:tl + 1]), "sdv")
            w = []
            if tl >= 2:
                w.append(("spe", pe_tr4[tl - 2]))
            dv_an = emit("vector", w,
                 lambda e, b2=b2, tl=tl: e.tensor_scalar(
                     antm[b2][:], acc[:, tl * H:(tl + 1) * H], sm_t1[:, 0:1], None, ALU.mult), "sdv")
            for fc in range(16):
                slot = fc % 2
                w = [("sdv", dv_an)]
                if fc >= 2:
                    w.append(("sac", ac_ev4[tl][fc - 2]))
                elif tl >= 1:
                    w.append(("sac", ac_ev4[tl - 1][14 + fc]))
                pe_v = emit("tensor", w,
                    lambda e, b2=b2, fc=fc, slot=slot: e.transpose(
                        ps_t4[slot][:],
                        antm[b2][:, fc * 128:(fc + 1) * 128], idb[:]), "spe")
                ac_ev4[tl][fc] = emit("scalar", [("spe", pe_v)],
                    lambda e, fc=fc, slot=slot, tl=tl: e.copy(
                        anout[:, fc * TPC + tl * 128: fc * TPC + (tl + 1) * 128],
                        ps_t4[slot][:]), "sac")
            pe_tr4[tl] = S["spe"]

        barrier()
        close_group("PS4A")
        close_group("P4A")

        g4b = open_group("P4B")
        xt7 = [g4b.enter_context(nc.sbuf_tensor(f"xt7{i}", [128, H], F32)) for i in range(2)]
        sft = [g4b.enter_context(nc.sbuf_tensor(f"sft{i}", [128, HC * 128], BF16)) for i in range(2)]
        zt = [g4b.enter_context(nc.sbuf_tensor(f"zt{i}", [128, H], F32)) for i in range(2)]
        ps4b = open_group("PS4B")
        ps_o = ps4b.enter_context(nc.psum_tensor("ps_o", [128, H], F32))
        ps_t7 = ps4b.enter_context(nc.psum_tensor("ps_t7", [128, H], BF16))

        ld7 = [0] * NT
        pe_o = [0] * NT
        pe_t7 = [0] * NT
        dv_zo = [0] * NT
        dv_zt = [0] * NT
        st_out = [0] * NT
        for tl in range(NT):
            b2 = tl % 2
            w = []
            if tl >= 2:
                w.append(("sdv", dv_zo[tl - 2]))
            emit("sync", w,
                 lambda e, b2=b2, tl=tl: e.dma_start(xt7[b2][:], x_in[tl * 128:(tl + 1) * 128, :]),
                 f"sx7{b2}", 16)
            w = []
            if tl >= 2:
                w.append(("spe", pe_t7[tl - 2]))
            ld7[tl] = emit("sync", w,
                 lambda e, b2=b2, tl=tl: e.dma_start(
                     sft[b2].rearrange("p (fc k) -> p fc k", fc=HC),
                     s_ffn[:, tl * 128:(tl + 1) * 128]
                     .rearrange("(fc p) k -> p fc k", p=128)), f"sx7{b2}", 16)
            for hc in range(HC):
                for n in range(4):
                    w = []
                    if hc == 0 and n == 0:
                        w.append(("swo", wo_ld))
                        if tl >= 1:
                            w.append(("sdv", dv_zo[tl - 1]))
                    last = (hc == HC - 1 and n == 3)
                    pe_o[tl] = emit("tensor", w,
                        lambda e, hc=hc, n=n, tl=tl: e.matmul(
                            ps_o[:, n * 512:(n + 1) * 512],
                            anout[:, hc * TPC + tl * 128: hc * TPC + (tl + 1) * 128],
                            wo_sb[:, hc * H + n * 512: hc * H + (n + 1) * 512],
                            start=(hc == 0), stop=(hc == HC - 1)),
                        "spe" if last else None)
            for fc in range(16):
                w = []
                if fc == 0:
                    w.append((f"sx7{b2}", ld7[tl]))
                    if tl >= 1:
                        w.append(("sdv", dv_zt[tl - 1]))
                pe_t7[tl] = emit("tensor", w,
                    lambda e, b2=b2, fc=fc: e.transpose(
                        ps_t7[:, fc * 128:(fc + 1) * 128],
                        sft[b2][:, fc * 128:(fc + 1) * 128], idb[:]),
                    "spe" if fc == 15 else None)
            w = [(f"sx7{b2}", ld7[tl])]
            if tl >= 2:
                w.append((f"sout{b2}", st_out[tl - 2]))
            emit("vector", w,
                 lambda e, b2=b2, tl=tl: e.tensor_scalar(
                     zt[b2][:], xt7[b2][:], a_a[:, tl:tl + 1], b_a[:, tl:tl + 1], ALU.mult, ALU.add), "sdv")
            dv_zo[tl] = emit("vector", [("spe", pe_o[tl])],
                 lambda e, b2=b2: e.tensor_tensor(zt[b2][:], zt[b2][:], ps_o[:], ALU.add), "sdv")
            dv_zt[tl] = emit("vector", [("spe", pe_t7[tl])],
                 lambda e, b2=b2: e.tensor_tensor(zt[b2][:], zt[b2][:], ps_t7[:], ALU.add), "sdv")
            st_out[tl] = emit("sync", [("sdv", dv_zt[tl])],
                lambda e, b2=b2, tl=tl: e.dma_start(
                    out_d[tl * 128:(tl + 1) * 128, :], zt[b2][:]), f"sout{b2}", 16)

        final_w = [("sout0", S["sout0"]), ("sout1", S["sout1"])]
        emit("sync", final_w, lambda e: e.nop())

        with ExitStack() as sem_stack:
            for nm in S:
                semh[nm] = sem_stack.enter_context(nc.semaphore(nm))
            with nc.Block() as block:
                @block.tensor
                def _(eng):
                    for f in ops["tensor"]:
                        f(eng)

                @block.vector
                def _(eng):
                    for f in ops["vector"]:
                        f(eng)

                @block.scalar
                def _(eng):
                    for f in ops["scalar"]:
                        f(eng)

                @block.gpsimd
                def _(eng):
                    for f in ops["gpsimd"]:
                        f(eng)

                @block.sync
                def _(eng):
                    for f in ops["sync"]:
                        f(eng)
        close_group("PS4B")
        close_group("P4B")
        close_group("WO")
        close_group("ACC")
    return nc


_PERM = (np.arange(I_FF) % 2048) * 4 + (np.arange(I_FF) // 2048)


def _build_pmask(r):
    own = owned_tiles(r)
    NEG = np.float32(-1.0e30)
    tri = np.where(np.tril(np.ones((128, 128), dtype=bool)), np.float32(0), NEG)
    keep = np.zeros((128, 128), dtype=np.float32)
    drop = np.full((128, 128), NEG, dtype=np.float32)
    m = np.empty((NCH, NT, 128, 4, 128), dtype=np.float32)
    for c in range(NCH):
        for tl in range(NT):
            t = own[tl]
            for kbl in range(4):
                kb = 4 * c + kbl
                m[c, tl, :, kbl, :] = keep if kb < t else (tri if kb == t else drop)
    return np.ascontiguousarray(
        m.reshape(NCH * NT * 128, 512)).astype(ml_dtypes.bfloat16)


def kernel(L0_attn_x, L0_attn_qkv, L0_attn_o_proj, L0_attn_gamma, L0_attn_beta,
           L0_ffn_w_gate, L0_ffn_w_up, L0_ffn_w_down):
    x = np.asarray(L0_attn_x)
    qkv = np.asarray(L0_attn_qkv, dtype=np.float32)
    wo = np.asarray(L0_attn_o_proj, dtype=np.float32)
    gamma = np.asarray(L0_attn_gamma, dtype=np.float32)
    beta = np.asarray(L0_attn_beta, dtype=np.float32)
    wg = np.asarray(L0_ffn_w_gate, dtype=np.float32)[:, _PERM]
    wu = np.asarray(L0_ffn_w_up, dtype=np.float32)[:, _PERM]
    wd = np.asarray(L0_ffn_w_down, dtype=np.float32)[_PERM, :]

    qkt = np.ascontiguousarray(
        qkv[:, :2 * H].reshape(HC, 128, 32, 128).transpose(2, 1, 0, 3).reshape(32 * 128, HC * 128))
    wvt = np.ascontiguousarray(
        qkv[:, 2 * H:].reshape(HC, 128, 4, 512).transpose(2, 1, 0, 3).reshape(4 * 128, HC * 512))
    wot = np.ascontiguousarray(
        wo.reshape(HC, 128, H).transpose(1, 0, 2).reshape(128, HC * H))
    wgt = np.ascontiguousarray(
        wg.reshape(HC, 128, 64, 128).transpose(2, 1, 0, 3).reshape(64 * 128, HC * 128))
    wut = np.ascontiguousarray(
        wu.reshape(HC, 128, 64, 128).transpose(2, 1, 0, 3).reshape(64 * 128, HC * 128))
    wdt = np.ascontiguousarray(
        wd.reshape(64, 128, HC, 128).transpose(2, 1, 0, 3).reshape(HC * 128, 64 * 128))

    gam2 = np.ascontiguousarray(gamma.reshape(HC, 128).T)
    bet2 = np.ascontiguousarray(beta.reshape(HC, 128).T)
    idf = np.eye(128, dtype=np.float32)
    idb = np.eye(128, dtype=np.float32).astype(ml_dtypes.bfloat16)

    in_maps = []
    idxs = []
    for r in range(NCORES):
        idx = np.r_[512 * r:512 * (r + 1), S_TOT - 512 * (r + 1):S_TOT - 512 * r]
        idxs.append(idx)
        in_maps.append(dict(
            x=np.ascontiguousarray(x[0, idx, :], dtype=np.float32),
            qkt=qkt, wvt=wvt, wot=wot, wgt=wgt, wut=wut, wdt=wdt,
            gam2=gam2, bet2=bet2, idf=idf, idb=idb,
            pmask=_build_pmask(r)))

    nc = build_core()
    res = run_bass_kernel_spmd(nc, in_maps, core_ids=list(range(NCORES)))
    out = np.empty((1, S_TOT, H), dtype=np.float32)
    for r in range(NCORES):
        out[0, idxs[r], :] = res.results[r]["out"]
    return out


# version 18
# speedup vs baseline: 1.0078x; 1.0078x over previous
import math
from contextlib import ExitStack

import numpy as np
import ml_dtypes

import concourse.bass as bass
import concourse.mybir as mybir
from concourse.bass_utils import run_bass_kernel_spmd

F32 = mybir.dt.float32
BF16 = mybir.dt.bfloat16
AF = mybir.ActivationFunctionType
ALU = mybir.AluOpType
AX = mybir.AxisListType

NCORES = 8
S_TOT = 8192
H = 2048
I_FF = 8192
TPC = 1024
NT = 8
HC = 16
NCH = 16
SIG = 1.0 / math.sqrt(2048.0)
EPS = 1e-5


def owned_tiles(r):
    return [4 * r + i for i in range(4)] + [60 - 4 * r + i for i in range(4)]


def chunk_rank_half(c):
    return (c, 0) if c <= 7 else (15 - c, 1)


def build_core():
    nc = bass.Bass("TRN2", target_bir_lowering=False)

    x_in = nc.dram_tensor("x", [TPC, H], F32, kind="ExternalInput").ap()
    qkt = nc.dram_tensor("qkt", [32 * 128, HC * 128], F32, kind="ExternalInput").ap()
    wvt = nc.dram_tensor("wvt", [4 * 128, HC * 512], F32, kind="ExternalInput").ap()
    wot = nc.dram_tensor("wot", [128, HC * H], F32, kind="ExternalInput").ap()
    wgt = nc.dram_tensor("wgt", [64 * 128, HC * 128], F32, kind="ExternalInput").ap()
    wut = nc.dram_tensor("wut", [64 * 128, HC * 128], F32, kind="ExternalInput").ap()
    wdt = nc.dram_tensor("wdt", [16 * 128, 64 * 128], F32, kind="ExternalInput").ap()
    gam2 = nc.dram_tensor("gam2", [128, HC], F32, kind="ExternalInput").ap()
    bet2 = nc.dram_tensor("bet2", [128, HC], F32, kind="ExternalInput").ap()
    idf_d = nc.dram_tensor("idf", [128, 128], F32, kind="ExternalInput").ap()
    idb_d = nc.dram_tensor("idb", [128, 128], BF16, kind="ExternalInput").ap()
    pmask_d = nc.dram_tensor("pmask", [NCH * NT * 128, 512], BF16, kind="ExternalInput").ap()
    out_d = nc.dram_tensor("out", [TPC, H], F32, kind="ExternalOutput").ap()

    k_loc = nc.dram_tensor("k_loc", [H, TPC], BF16, kind="Internal").ap()
    v_loc = nc.dram_tensor("v_loc", [TPC, H], BF16, kind="Internal").ap()
    q_dram = nc.dram_tensor("q_dram", [H, TPC], BF16, kind="Internal").ap()
    k_gath = nc.dram_tensor("k_gath", [NCORES * H, TPC], BF16, kind="Internal").ap()
    v_gath = nc.dram_tensor("v_gath", [NCORES * TPC, H], BF16, kind="Internal").ap()
    s_ffn = nc.dram_tensor("s_ffn", [H, TPC], BF16, kind="Internal").ap()

    ENGS = ("tensor", "vector", "scalar", "gpsimd", "sync")
    ops = {E: [] for E in ENGS}
    S = {"scst": 0, "sx0": 0, "sx1": 0,
         "swq0": 0, "swq1": 0, "swv0": 0, "swv1": 0,
         "skst0": 0, "skst1": 0, "svst0": 0, "svst1": 0, "sqst0": 0, "sqst1": 0,
         "skv0": 0, "skv1": 0, "spm0": 0, "spm1": 0,
         "sql": 0, "swo": 0,
         "swg0": 0, "swg1": 0, "swu0": 0, "swu1": 0,
         "swd0": 0, "swd1": 0, "sfst0": 0, "sfst1": 0,
         "sx70": 0, "sx71": 0, "sout0": 0, "sout1": 0,
         "scc": 0, "spe": 0, "sdv": 0, "sac": 0, "spl": 0}
    semh = {}
    pend = {E: [] for E in ENGS}

    OWN = {"vector": "sdv", "scalar": "sac"}

    def emit(E, waits, fn, inc=None, n=1):
        w = list(pend[E]) + [(nm, v) for nm, v in waits if v > 0]
        if E in OWN and S[OWN[E]] > 0:
            w.append((OWN[E], S[OWN[E]]))
        pend[E].clear()
        if inc is not None:
            S[inc] += n
        def run(e, w=tuple(w), fn=fn, inc=inc, n=n):
            for nm, v in w:
                e.wait_ge(semh[nm], v)
            rr = fn(e)
            if inc is not None:
                rr.then_inc(semh[inc], n)
        ops[E].append(run)
        return S[inc] if inc is not None else None

    def barrier():
        snap = [(nm, v) for nm, v in S.items() if v > 0 and nm != "scc"]
        for E in ENGS:
            pend[E] = list(snap)

    stacks = {}

    def open_group(name):
        stacks[name] = ExitStack()
        return stacks[name]

    def close_group(name):
        stacks[name].close()
        del stacks[name]

    outer = ExitStack()
    with outer:
        e_ = outer.enter_context
        idf = e_(nc.sbuf_tensor("idf_s", [128, 128], F32))
        idb = e_(nc.sbuf_tensor("idb_s", [128, 128], BF16))
        gam = e_(nc.sbuf_tensor("gam_s", [128, HC], F32))
        bet = e_(nc.sbuf_tensor("bet_s", [128, HC], F32))
        epsc = e_(nc.sbuf_tensor("eps_s", [128, 1], F32))
        eps2c = e_(nc.sbuf_tensor("eps2_s", [128, 1], F32))
        mu_a = e_(nc.sbuf_tensor("mu_a", [128, NT], F32))
        rstd_a = e_(nc.sbuf_tensor("rstd_a", [128, NT], F32))
        a_a = e_(nc.sbuf_tensor("a_a", [128, NT], F32))
        b_a = e_(nc.sbuf_tensor("b_a", [128, NT], F32))
        sm_m = e_(nc.sbuf_tensor("sm_m", [128, 2 * NT], F32))
        sm_l = e_(nc.sbuf_tensor("sm_l", [128, NT], F32))
        sm_al = e_(nc.sbuf_tensor("sm_al", [128, NT], F32))
        sm_d = e_(nc.sbuf_tensor("sm_d", [128, NT], F32))
        sm_sc = e_(nc.sbuf_tensor("sm_sc", [128, NT], F32))
        sm_t1 = e_(nc.sbuf_tensor("sm_t1", [128, NT], F32))

        emit("sync", [], lambda e: e.dma_start(idf[:], idf_d[:]), "scst", 16)
        emit("sync", [], lambda e: e.dma_start(idb[:], idb_d[:]), "scst", 16)
        emit("sync", [], lambda e: e.dma_start(gam[:], gam2[:]), "scst", 16)
        emit("sync", [], lambda e: e.dma_start(bet[:], bet2[:]), "scst", 16)
        consts_done = S["scst"]
        emit("vector", [], lambda e: e.memset(epsc[:], EPS), "sdv")
        emit("vector", [], lambda e: e.memset(eps2c[:], EPS * EPS), "sdv")


        gXN = open_group("XN")
        xn_fm = gXN.enter_context(nc.sbuf_tensor("xn_fm", [128, HC * TPC], BF16))
        gA1 = open_group("A1")
        xnorm = gA1.enter_context(nc.sbuf_tensor("xnorm", [128, HC * TPC], BF16))
        g2 = open_group("P2")
        wqk = [g2.enter_context(nc.sbuf_tensor(f"wqk{i}", [128, HC * 128], BF16)) for i in range(2)]
        wvb = [g2.enter_context(nc.sbuf_tensor(f"wvb{i}", [128, HC * 512], BF16)) for i in range(2)]
        qkev = [g2.enter_context(nc.sbuf_tensor(f"qkev{i}", [128, TPC], BF16)) for i in range(2)]
        vev = [g2.enter_context(nc.sbuf_tensor(f"vev{i}", [128, 512], BF16)) for i in range(2)]
        ps2 = open_group("PS2")
        ps_qk = [ps2.enter_context(nc.psum_tensor(f"ps_qk{i}", [128, 1024], F32)) for i in range(2)]
        ps_v = [ps2.enter_context(nc.psum_tensor(f"ps_v{i}", [128, 512], F32)) for i in range(2)]
        g1 = open_group("P1")
        xt = [g1.enter_context(nc.sbuf_tensor(f"xt{i}", [128, H], F32)) for i in range(2)]
        xct = [g1.enter_context(nc.sbuf_tensor(f"xct{i}", [128, H], F32)) for i in range(2)]
        xnt = [g1.enter_context(nc.sbuf_tensor(f"xnt{i}", [128, H], F32)) for i in range(2)]
        yt = [g1.enter_context(nc.sbuf_tensor(f"yt{i}", [128, H], F32)) for i in range(2)]
        sqt = g1.enter_context(nc.sbuf_tensor("sqt", [128, H], F32))
        vr = g1.enter_context(nc.sbuf_tensor("vr", [128, 6], F32))
        R_a = g1.enter_context(nc.sbuf_tensor("R_a", [128, NT], F32))
        ps1 = open_group("PS1")
        ps_tr = [ps1.enter_context(nc.psum_tensor(f"ps_tr{i}", [128, 128], F32)) for i in range(2)]

        x_loaded = [0] * NT
        dv_tile = [0] * NT
        pe_trdone = [0] * NT
        ev_done = [[("sdv", 0)] * 32 for _ in range(NT)]
        dv_sub = [0] * NT
        dv_var = [0] * NT
        dv_xn = [0] * NT
        ac_sq_l = [0] * NT
        ac_exp2 = [0] * NT

        def stageA(tl):
            b2 = tl % 2
            w = [("scst", consts_done)]
            if tl >= 2:
                w.append(("sdv", dv_tile[tl - 2]))
            x_loaded[tl] = emit("sync", w,
                lambda e, tl=tl, b2=b2: e.dma_start(xt[b2][:], x_in[tl * 128:(tl + 1) * 128, :]),
                f"sx{b2}", 16)
            w = [(f"sx{b2}", x_loaded[tl])]
            if tl >= 2:
                w += [ev_done[tl - 2][31], ev_done[tl - 2][15]]
            emit("vector", w,
                 lambda e, b2=b2: e.reduce_sum(vr[:, 0:1], xt[b2][:], axis=AX.X), "sdv")
            emit("vector", [],
                 lambda e, tl=tl: e.tensor_scalar_mul(mu_a[:, tl:tl + 1], vr[:, 0:1], 1.0 / H), "sdv")
            dv_sub[tl] = emit("vector", [],
                 lambda e, b2=b2, tl=tl: e.tensor_scalar(xct[b2][:], xt[b2][:], mu_a[:, tl:tl + 1], None, ALU.subtract),
                 "sdv")

        def stageB(tl):
            w = [("sdv", dv_sub[tl])]
            if tl >= 1:
                w.append(("sdv", dv_var[tl - 1]))
            ac_sq_l[tl] = emit("scalar", w,
                 lambda e, b2=tl % 2: e.activation(sqt[:], xct[b2][:], AF.Square, accum_out=vr[:, 1:2]), "sac")

        def stageC(tl):
            emit("vector", [("sac", ac_sq_l[tl])],
                 lambda e: e.tensor_scalar_mul(vr[:, 4:5], vr[:, 1:2], (1.0 + EPS) / H), "sdv")
            dv_var[tl] = emit("vector", [],
                 lambda e: e.tensor_scalar_mul(vr[:, 1:2], vr[:, 1:2], 1.0 / H), "sdv")

        def stageL(tl):
            emit("scalar", [("sdv", dv_var[tl])],
                 lambda e: e.activation(vr[:, 2:3], vr[:, 1:2], AF.Ln, bias=epsc[:, 0:1]), "sac")
            emit("scalar", [],
                 lambda e, tl=tl: e.activation(rstd_a[:, tl:tl + 1], vr[:, 2:3], AF.Exp, scale=-0.5), "sac")
            emit("scalar", [],
                 lambda e: e.activation(vr[:, 5:6], vr[:, 4:5], AF.Ln, bias=eps2c[:, 0:1]), "sac")
            ac_exp2[tl] = emit("scalar", [],
                 lambda e, tl=tl: e.activation(R_a[:, tl:tl + 1], vr[:, 5:6], AF.Exp, scale=-0.5), "sac")

        def stageD(tl):
            b2 = tl % 2
            w = [("sac", ac_exp2[tl])]
            if tl >= 2:
                w.append(("spe", pe_trdone[tl - 2]))
            dv_xn[tl] = emit("vector", w,
                 lambda e, b2=b2, tl=tl: e.tensor_scalar(xnt[b2][:], xct[b2][:], rstd_a[:, tl:tl + 1], None, ALU.mult),
                 "sdv")
            dv_tile[tl] = emit("vector", [],
                 lambda e, b2=b2, tl=tl: e.tensor_scalar(yt[b2][:], xct[b2][:], R_a[:, tl:tl + 1], None, ALU.mult),
                 "sdv")

        def stageT(tl):
            b2 = tl % 2
            for idx in range(32):
                is_y = idx >= 16
                fc = idx % 16
                slot = idx % 2
                src = yt[b2] if is_y else xnt[b2]
                w = [("sdv", dv_tile[tl] if is_y else dv_xn[tl])]
                if idx >= 2:
                    w.append(ev_done[tl][idx - 2])
                elif tl >= 2:
                    w.append(ev_done[tl - 2][30 + idx])
                pe_v = emit("tensor", w,
                    lambda e, src=src, fc=fc, slot=slot:
                        e.transpose(ps_tr[slot][:], src[:, fc * 128:(fc + 1) * 128], idf[:]),
                    "spe")
                if is_y:
                    def ev(e, fc=fc, slot=slot, tl=tl):
                        return e.tensor_scalar(
                            xnorm[:, fc * TPC + tl * 128: fc * TPC + (tl + 1) * 128],
                            ps_tr[slot][:],
                            gam[:, fc:fc + 1], bet[:, fc:fc + 1], ALU.mult, ALU.add)
                    ev_done[tl][idx] = ("sdv", emit("vector", [("spe", pe_v)], ev, "sdv"))
                else:
                    def ev(e, fc=fc, slot=slot, tl=tl):
                        return e.copy(xn_fm[:, fc * TPC + tl * 128: fc * TPC + (tl + 1) * 128],
                                      ps_tr[slot][:])
                    ev_done[tl][idx] = ("sac", emit("scalar", [("spe", pe_v)], ev, "sac"))
            pe_trdone[tl] = S["spe"]

        stageA(0)
        stageB(0)
        stageA(1)
        for tl in range(NT):
            stageC(tl)
            stageL(tl)
            if tl + 1 < NT:
                stageB(tl + 1)
            stageD(tl)
            stageT(tl)
            if tl + 2 < NT:
                stageA(tl + 2)

        emit("vector", [("sac", S["sac"])],
             lambda e: e.tensor_scalar(a_a[:], rstd_a[:], 1.0, None, ALU.add), "sdv")
        emit("vector", [],
             lambda e: e.tensor_tensor(b_a[:], mu_a[:], rstd_a[:], ALU.mult), "sdv")
        emit("vector", [],
             lambda e: e.tensor_scalar_mul(b_a[:], b_a[:], -1.0), "sdv")

        close_group("PS1")
        close_group("P1")

        pe_vp = [0] * 32
        ac_vp = [0] * 32
        st_v = [0] * 32
        for cb in range(4):
            b2 = cb % 2
            w = []
            if cb >= 2:
                w.append(("spe", pe_vp[(cb - 2) * NT + NT - 1]))
            wldv = emit("gpsimd", w,
                lambda e, b2=b2, cb=cb: e.dma_start(
                    wvb[b2][:], wvt[cb * 128:(cb + 1) * 128, :]),
                f"swv{b2}", 16)
            for tl in range(NT):
                vi = cb * NT + tl
                pb = vi % 2
                for hc in range(HC):
                    w = []
                    if hc == 0:
                        w.append((f"swv{b2}", wldv))
                        w.append(ev_done[tl][31])
                        if vi >= 2:
                            w.append(("sac", ac_vp[vi - 2]))
                    pe_vp[vi] = emit("tensor", w,
                        lambda e, pb=pb, b2=b2, hc=hc, tl=tl: e.matmul(
                            ps_v[pb][:],
                            xnorm[:, hc * TPC + tl * 128: hc * TPC + (tl + 1) * 128],
                            wvb[b2][:, hc * 512:(hc + 1) * 512],
                            start=(hc == 0), stop=(hc == HC - 1)),
                        "spe" if hc == HC - 1 else None)
                w = [("spe", pe_vp[vi])]
                if vi >= 2:
                    w.append((f"svst{pb}", st_v[vi - 2]))
                ac_vp[vi] = emit("scalar", w,
                    lambda e, pb=pb: e.copy(vev[pb][:], ps_v[pb][:]), "sac")
                st_v[vi] = emit("sync", [("sac", ac_vp[vi])],
                    lambda e, pb=pb, tl=tl, cb=cb: e.dma_start(
                        v_loc[tl * 128:(tl + 1) * 128, cb * 512:(cb + 1) * 512], vev[pb][:]),
                    f"svst{pb}", 16)

        emit("gpsimd", [("svst0", S["svst0"]), ("svst1", S["svst1"])],
            lambda e: e.collective_compute("AllGather", ALU.bypass,
                replica_groups=[list(range(NCORES))],
                ins=[v_loc.opt()], outs=[v_gath.opt()]), "scc", 1)

        pe_qk = [0] * 32
        ac_qk = [0] * 32
        st_q = [0] * 16
        st_k = [0] * 16
        for m in range(32):
            is_q = m < 16
            fc = m % 16
            b2 = m % 2
            w = []
            if m >= 2:
                w.append(("spe", pe_qk[m - 2]))
            wld = emit("gpsimd", w,
                lambda e, b2=b2, m=m: e.dma_start(
                    wqk[b2][:], qkt[m * 128:(m + 1) * 128, :]),
                f"swq{b2}", 16)
            for hc in range(HC):
                for n in range(2):
                    w = []
                    if hc == 0 and n == 0:
                        w.append((f"swq{b2}", wld))
                        if m == 0:
                            w.append(ev_done[NT - 1][31])
                        if m >= 2:
                            w.append(("sac", ac_qk[m - 2]))
                    last = (hc == HC - 1 and n == 1)
                    pe_qk[m] = emit("tensor", w,
                        lambda e, b2=b2, hc=hc, n=n: e.matmul(
                            ps_qk[b2][:, n * 512:(n + 1) * 512],
                            wqk[b2][:, hc * 128:(hc + 1) * 128],
                            xnorm[:, hc * TPC + n * 512: hc * TPC + (n + 1) * 512],
                            start=(hc == 0), stop=(hc == HC - 1)),
                        "spe" if last else None)
            sem = f"sqst{b2}" if is_q else f"skst{b2}"
            starr = st_q if is_q else st_k
            w = [("spe", pe_qk[m])]
            if fc >= 2:
                w.append((sem, starr[fc - 2]))
            ac_qk[m] = emit("scalar", w,
                lambda e, b2=b2: e.copy(qkev[b2][:], ps_qk[b2][:]), "sac")
            dst = q_dram if is_q else k_loc
            starr[fc] = emit("sync", [("sac", ac_qk[m])],
                lambda e, b2=b2, fc=fc, dst=dst: e.dma_start(dst[fc * 128:(fc + 1) * 128, :], qkev[b2][:]),
                sem, 16)

        emit("gpsimd", [("skst0", S["skst0"]), ("skst1", S["skst1"])],
            lambda e: e.collective_compute("AllGather", ALU.bypass,
                replica_groups=[list(range(NCORES))],
                ins=[k_loc.opt()], outs=[k_gath.opt()]), "scc", 1)

        barrier()
        close_group("PS2")
        close_group("P2")
        close_group("A1")

        g5o = open_group("HS")
        h_s = g5o.enter_context(nc.sbuf_tensor("h_s", [128, 64 * TPC], BF16))
        g5 = open_group("P5")
        wgb = [g5.enter_context(nc.sbuf_tensor(f"wgb{i}", [128, HC * 128], BF16)) for i in range(2)]
        wub = [g5.enter_context(nc.sbuf_tensor(f"wub{i}", [128, HC * 128], BF16)) for i in range(2)]
        gsb = [g5.enter_context(nc.sbuf_tensor(f"gsb{i}", [128, 512], F32)) for i in range(2)]
        hjt = [g5.enter_context(nc.sbuf_tensor(f"hjt{j}", [128, TPC], F32)) for j in range(4)]
        sjt = [g5.enter_context(nc.sbuf_tensor(f"sjt{j}", [128, 512], F32)) for j in range(4)]
        aux = [g5.enter_context(nc.sbuf_tensor(f"aux{j}", [128, 512], F32)) for j in range(1)]
        ps5 = open_group("PS5")
        ps_g = [ps5.enter_context(nc.psum_tensor(f"ps_g{i}", [128, 1024], F32)) for i in range(2)]
        ps_u = [ps5.enter_context(nc.psum_tensor(f"ps_u{i}", [128, 1024], F32)) for i in range(2)]

        pe_gu = {}
        dv_h = {}
        ac_silu = {}
        last_dvh = 0
        prev_h = {0: None, 1: None}
        for b in range(16):
            for j in range(4):
                ic = 16 * j + b
                q = b * 4 + j
                pp = q % 2
                w = []
                if q >= 2:
                    w.append(("spe", pe_gu[q - 2]))
                wldg = emit("gpsimd", w,
                    lambda e, pp=pp, ic=ic: e.dma_start(
                        wgb[pp][:], wgt[ic * 128:(ic + 1) * 128, :]),
                    f"swg{pp}", 16)
                wld = emit("gpsimd", [],
                    lambda e, pp=pp, ic=ic: e.dma_start(
                        wub[pp][:], wut[ic * 128:(ic + 1) * 128, :]),
                    f"swu{pp}", 16)
                for which in range(2):
                    wsrc = wgb[pp] if which == 0 else wub[pp]
                    pdst = ps_g[pp] if which == 0 else ps_u[pp]
                    for hc in range(HC):
                        for n in range(2):
                            w = []
                            if which == 0 and hc == 0 and n == 0:
                                w.append((f"swg{pp}", wldg))
                                w.append((f"swu{pp}", wld))
                                if q >= 2:
                                    w.append(("sdv", dv_h[(q - 2, 1)]))
                            last = (which == 1 and hc == HC - 1 and n == 1)
                            pe_gu[q] = emit("tensor", w,
                                lambda e, wsrc=wsrc, pdst=pdst, hc=hc, n=n: e.matmul(
                                    pdst[:, n * 512:(n + 1) * 512],
                                    wsrc[:, hc * 128:(hc + 1) * 128],
                                    xn_fm[:, hc * TPC + n * 512: hc * TPC + (n + 1) * 512],
                                    start=(hc == 0), stop=(hc == HC - 1)),
                                "spe" if last else None)
                for th in range(2):
                    w = [("spe", pe_gu[q])]
                    if prev_h[th] is not None:
                        w.append(("sdv", prev_h[th]))
                    ac_silu[(q, th)] = emit("scalar", w,
                        lambda e, pp=pp, th=th: e.activation(
                            gsb[th][:], ps_g[pp][:, th * 512:(th + 1) * 512], AF.Silu), "sac")
                    dv_h[(q, th)] = emit("vector", [("sac", ac_silu[(q, th)])],
                        lambda e, pp=pp, th=th, j=j: e.tensor_tensor(
                            hjt[j][:, th * 512:(th + 1) * 512], gsb[th][:],
                            ps_u[pp][:, th * 512:(th + 1) * 512], ALU.mult), "sdv")
                    prev_h[th] = dv_h[(q, th)]
                    last_dvh = dv_h[(q, th)]
                    if j == 3:
                        sl = slice(th * 512, (th + 1) * 512)
                        for jj in range(4):
                            emit("vector", [],
                                 lambda e, jj=jj, sl=sl: e.tensor_tensor(
                                     sjt[jj][:], hjt[jj][:, sl], hjt[jj][:, sl], ALU.mult), "sdv")
                        emit("vector", [], lambda e: e.tensor_tensor(aux[0][:], sjt[0][:], sjt[1][:], ALU.max), "sdv")
                        emit("vector", [], lambda e: e.tensor_tensor(sjt[0][:], sjt[0][:], sjt[1][:], ALU.min), "sdv")
                        emit("vector", [], lambda e: e.tensor_tensor(sjt[1][:], sjt[2][:], sjt[3][:], ALU.max), "sdv")
                        emit("vector", [], lambda e: e.tensor_tensor(sjt[2][:], sjt[2][:], sjt[3][:], ALU.min), "sdv")
                        emit("vector", [], lambda e: e.tensor_tensor(aux[0][:], aux[0][:], sjt[1][:], ALU.min), "sdv")
                        emit("vector", [], lambda e: e.tensor_tensor(sjt[0][:], sjt[0][:], sjt[2][:], ALU.max), "sdv")
                        emit("vector", [], lambda e: e.tensor_tensor(aux[0][:], aux[0][:], sjt[0][:], ALU.max), "sdv")
                        for jj in range(4):
                            ic2 = 16 * jj + b
                            emit("vector", [],
                                 lambda e, jj=jj, sl=sl: e.tensor_tensor(
                                     sjt[1][:], hjt[jj][:, sl], hjt[jj][:, sl], ALU.mult), "sdv")
                            emit("vector", [],
                                 lambda e: e.tensor_tensor(sjt[1][:], sjt[1][:], aux[0][:], ALU.is_ge), "sdv")
                            last_dvh = emit("vector", [],
                                 lambda e, jj=jj, sl=sl, ic2=ic2, th=th: e.tensor_tensor(
                                     h_s[:, ic2 * TPC + th * 512: ic2 * TPC + (th + 1) * 512],
                                     hjt[jj][:, sl], sjt[1][:], ALU.mult), "sdv")

        barrier()
        close_group("PS5")
        close_group("P5")

        g6 = open_group("P6")
        wdb = [g6.enter_context(nc.sbuf_tensor(f"wdb{i}", [128, 64 * 128], BF16)) for i in range(2)]
        sfev = [g6.enter_context(nc.sbuf_tensor(f"sfev{i}", [128, TPC], BF16)) for i in range(2)]
        ps6 = open_group("PS6")
        ps_f = [ps6.enter_context(nc.psum_tensor(f"ps_f{i}", [128, 1024], F32)) for i in range(2)]
        pe_f = [0] * 16
        ac_f = [0] * 16
        st_f = [0] * 16
        for fc in range(16):
            pb = fc % 2
            w = []
            if fc >= 2:
                w.append(("spe", pe_f[fc - 2]))
            wldd = emit("gpsimd", w,
                lambda e, pb=pb, fc=fc: e.dma_start(
                    wdb[pb][:], wdt[fc * 128:(fc + 1) * 128, :]),
                f"swd{pb}", 16)
            for ic in range(64):
                for n in range(2):
                    w = []
                    if ic == 0 and n == 0:
                        w.append((f"swd{pb}", wldd))
                        if fc >= 2:
                            w.append(("sac", ac_f[fc - 2]))
                    last = (ic == 63 and n == 1)
                    pe_f[fc] = emit("tensor", w,
                        lambda e, pb=pb, ic=ic, n=n: e.matmul(
                            ps_f[pb][:, n * 512:(n + 1) * 512],
                            wdb[pb][:, ic * 128:(ic + 1) * 128],
                            h_s[:, ic * TPC + n * 512: ic * TPC + (n + 1) * 512],
                            start=(ic == 0), stop=(ic == 63)),
                        "spe" if last else None)
            w = [("spe", pe_f[fc])]
            if fc >= 2:
                w.append((f"sfst{pb}", st_f[fc - 2]))
            ac_f[fc] = emit("scalar", w,
                lambda e, pb=pb: e.copy(sfev[pb][:], ps_f[pb][:]), "sac")
            st_f[fc] = emit("sync", [("sac", ac_f[fc])],
                lambda e, pb=pb, fc=fc: e.dma_start(
                    s_ffn[fc * 128:(fc + 1) * 128, :], sfev[pb][:]), f"sfst{pb}", 16)

        barrier()
        close_group("PS6")
        close_group("P6")
        close_group("HS")
        close_group("XN")

        gacc = open_group("ACC")
        acc = gacc.enter_context(nc.sbuf_tensor("acc", [128, NT * H], F32))
        g3 = open_group("P3")
        q_fm = g3.enter_context(nc.sbuf_tensor("q_fm", [128, HC * TPC], BF16))
        kbuf = [g3.enter_context(nc.sbuf_tensor(f"kbuf{i}", [128, HC * 512], BF16)) for i in range(2)]
        vbuf = [g3.enter_context(nc.sbuf_tensor(f"vbuf{i}", [128, 4 * H], BF16)) for i in range(2)]
        p_sb = [g3.enter_context(nc.sbuf_tensor(f"p_sb{i}", [128, 512], BF16)) for i in range(2)]
        pt_sb = [g3.enter_context(nc.sbuf_tensor(f"pt_sb{i}", [128, 512], BF16)) for i in range(2)]
        pm_sb = [g3.enter_context(nc.sbuf_tensor(f"pm_sb{i}", [128, NT * 512], BF16)) for i in range(2)]
        ps3 = open_group("PS3")
        ps_s = [ps3.enter_context(nc.psum_tensor(f"ps_s{i}", [128, 512], F32)) for i in range(2)]
        ps_pt = [ps3.enter_context(nc.psum_tensor(f"ps_pt{i}", [128, 512], BF16)) for i in range(2)]
        ps_pv = ps3.enter_context(nc.psum_tensor("ps_pv", [128, H], F32))

        q_ld = emit("sync", [],
            lambda e: e.dma_start(q_fm.rearrange("p (fc t) -> p fc t", fc=HC),
                                  q_dram.rearrange("(fc p) t -> p fc t", p=128)), "sql", 16)

        NP = NCH * NT
        kv_ld = [0] * NCH
        pm_ld = [0] * NCH
        pe_sc = [0] * NP
        pe_tr3 = [0] * NP
        pe_pv3 = [0] * NP
        dv_pmul = [0] * NP
        dv_rsum = [0] * NP
        dv_state = [0] * NP
        dv_ptev = [0] * NP
        dv_accu = [0] * NP
        dv_lup = [0] * NP
        ac_al = [0] * NP
        ac_ex = [0] * NP
        pe_chunk_end = [0] * NCH

        def emit_kv_load(c):
            rr, hh = chunk_rank_half(c)
            w = [("scc", 1)]
            if c >= 2:
                w.append(("spe", pe_chunk_end[c - 2]))
            emit("sync", w,
                lambda e, rr=rr, hh=hh, c=c: e.dma_start(
                    vbuf[c % 2].rearrange("p (kb f) -> p kb f", kb=4),
                    v_gath[rr * TPC + hh * 512: rr * TPC + (hh + 1) * 512, :]
                    .rearrange("(kb p) f -> p kb f", p=128)), f"skv{c % 2}", 16)
            kv_ld[c] = emit("sync", [("scc", 2)],
                lambda e, rr=rr, hh=hh, c=c: e.dma_start(
                    kbuf[c % 2].rearrange("p (fc k) -> p fc k", fc=HC),
                    k_gath[rr * H:(rr + 1) * H, hh * 512:(hh + 1) * 512]
                    .rearrange("(fc p) k -> p fc k", p=128)), f"skv{c % 2}", 16)

        def emit_pm_load(c):
            w = []
            if c >= 2:
                w.append(("spe", pe_chunk_end[c - 2]))
            pm_ld[c] = emit("scalar", w,
                lambda e, c=c: e.dma_start(
                    pm_sb[c % 2].rearrange("p (tl k) -> p tl k", tl=NT),
                    pmask_d[c * NT * 128:(c + 1) * NT * 128, :]
                    .rearrange("(tl p) k -> p tl k", p=128)),
                f"spm{c % 2}", 16)

        def emit_scores(i):
            c, tl = divmod(i, NT)
            sb = i % 2
            for fc in range(HC):
                w = []
                if fc == 0:
                    w.append((f"skv{c % 2}", kv_ld[c]))
                    w.append(("sql", q_ld))
                    if i >= 2:
                        w.append(("sac", ac_ex[i - 2]))
                emit("tensor", w,
                    lambda e, sb=sb, fc=fc, tl=tl, c=c: e.matmul(
                        ps_s[sb][:, :],
                        q_fm[:, fc * TPC + tl * 128: fc * TPC + (tl + 1) * 128],
                        kbuf[c % 2][:, fc * 512:(fc + 1) * 512],
                        start=(fc == 0), stop=False),
                    None)
            pe_sc[i] = emit("tensor", [(f"spm{c % 2}", pm_ld[c])],
                lambda e, sb=sb, c=c, tl=tl: e.matmul(
                    ps_s[sb][:, :], idb[:],
                    pm_sb[c % 2][:, tl * 512:(tl + 1) * 512],
                    start=False, stop=True),
                "spe")

        def emit_softmax(i):
            c, tl = divmod(i, NT)
            sb = i % 2
            first = (c == 0)
            mo = (c % 2) * NT + tl
            mp = ((c - 1) % 2) * NT + tl
            emit("vector", [("spe", pe_sc[i])],
                 lambda e, sb=sb: e.reduce_max(sm_t1[:, 0:1], ps_s[sb][:, :],
                                               axis=AX.X, negate=True), "sdv")
            if first:
                dv_state[i] = emit("vector", [],
                    lambda e, mo=mo: e.tensor_scalar_mul(sm_m[:, mo:mo + 1], sm_t1[:, 0:1], SIG), "sdv")
            else:
                dv_state[i] = emit("vector", [],
                    lambda e, mo=mo, mp=mp: e.tensor_scalar(
                        sm_m[:, mo:mo + 1], sm_t1[:, 0:1], SIG, sm_m[:, mp:mp + 1],
                        ALU.mult, ALU.min), "sdv")
            w = [("sdv", dv_state[i])]
            if i >= 2:
                w.append(("spe", pe_tr3[i - 2]))
            ac_ex[i] = emit("scalar", w,
                lambda e, sb=sb, pb=i % 2, mo=mo, tl=tl: e.activation(
                    p_sb[pb][:], ps_s[sb][:], AF.Exp,
                    bias=sm_m[:, mo:mo + 1], scale=SIG,
                    accum_out=sm_sc[:, tl:tl + 1]), "sac")
            dv_pmul[i] = 0
            dv_rsum[i] = 0

        def emit_alpha(i):
            c, tl = divmod(i, NT)
            if c == 0:
                return
            mo = (c % 2) * NT + tl
            mp = ((c - 1) % 2) * NT + tl
            dv_d = emit("vector", [],
                lambda e, mo=mo, mp=mp, tl=tl: e.tensor_tensor(
                    sm_d[:, tl:tl + 1], sm_m[:, mo:mo + 1], sm_m[:, mp:mp + 1], ALU.subtract), "sdv")
            ac_al[i] = emit("scalar", [("sdv", dv_d)],
                lambda e, tl=tl: e.activation(sm_al[:, tl:tl + 1], sm_d[:, tl:tl + 1], AF.Exp), "sac")

        def emit_tr(i):
            c, tl = divmod(i, NT)
            pb = i % 2
            for kb in range(4):
                w = []
                if kb == 0:
                    w.append(("sac", ac_ex[i]))
                    if i >= 2:
                        w.append(("sac", dv_ptev[i - 2]))
                pe_tr3[i] = emit("tensor", w,
                    lambda e, pb=pb, kb=kb: e.transpose(
                        ps_pt[pb][:, kb * 128:(kb + 1) * 128],
                        p_sb[pb][:, kb * 128:(kb + 1) * 128], idb[:]),
                    "spe" if kb == 3 else None)
            w = [("spe", pe_tr3[i])]
            if i >= 2:
                w.append(("spe", pe_pv3[i - 2]))
            dv_ptev[i] = emit("scalar", w,
                lambda e, pb=pb: e.copy(pt_sb[pb][:], ps_pt[pb][:]), "sac")

        def emit_pv(i):
            c, tl = divmod(i, NT)
            sb = i % 2
            first = (c == 0)
            for kb in range(4):
                for n in range(4):
                    w = []
                    if kb == 0 and n == 0:
                        w.append(("sac", dv_ptev[i]))
                        if i >= 1:
                            w.append(("sdv", dv_accu[i - 1]))
                    last = (kb == 3 and n == 3)
                    pe_pv3[i] = emit("tensor", w,
                        lambda e, sb=sb, kb=kb, n=n, c=c: e.matmul(
                            ps_pv[:, n * 512:(n + 1) * 512],
                            pt_sb[sb][:, kb * 128:(kb + 1) * 128],
                            vbuf[c % 2][:, kb * H + n * 512: kb * H + (n + 1) * 512],
                            start=(kb == 0), stop=(kb == 3)),
                        "spe" if last else None)
            w = [("spe", pe_pv3[i])]
            if not first:
                w.append(("sac", ac_al[i]))
            if first:
                dv_accu[i] = emit("vector", w,
                    lambda e, tl=tl: e.tensor_copy(acc[:, tl * H:(tl + 1) * H], ps_pv[:]), "sdv")
                dv_lup[i] = emit("vector", [("sac", ac_ex[i])],
                    lambda e, tl=tl: e.tensor_copy(sm_l[:, tl:tl + 1], sm_sc[:, tl:tl + 1]), "sdv")
            else:
                dv_accu[i] = emit("vector", w,
                    lambda e, tl=tl: e.scalar_tensor_tensor(
                        acc[:, tl * H:(tl + 1) * H], acc[:, tl * H:(tl + 1) * H],
                        sm_al[:, tl:tl + 1], ps_pv[:], ALU.mult, ALU.add), "sdv")
                dv_lup[i] = emit("vector", [("sac", ac_ex[i])],
                    lambda e, tl=tl: e.scalar_tensor_tensor(
                        sm_l[:, tl:tl + 1], sm_l[:, tl:tl + 1],
                        sm_al[:, tl:tl + 1], sm_sc[:, tl:tl + 1], ALU.mult, ALU.add), "sdv")

        emit_kv_load(0)
        emit_pm_load(0)
        emit_kv_load(1)
        emit_pm_load(1)
        emit_scores(0)
        emit_softmax(0)
        emit_scores(1)
        for i in range(NP):
            ci4, tl4 = divmod(i + 5, NT)
            if i + 5 < NP and tl4 == 0 and ci4 >= 2:
                pe_chunk_end[ci4 - 2] = S["spe"]
                emit_kv_load(ci4)
                emit_pm_load(ci4)
            if i + 1 < NP:
                emit_softmax(i + 1)
            emit_tr(i)
            if i + 1 < NP:
                emit_alpha(i + 1)
            if i + 2 < NP:
                emit_scores(i + 2)
            emit_pv(i)
        pe_chunk_end[NCH - 1] = S["spe"]

        barrier()
        close_group("PS3")
        close_group("P3")

        gWO = open_group("WO")
        wo_sb = gWO.enter_context(nc.sbuf_tensor("wo_sb", [128, HC * H], BF16))
        anout = gWO.enter_context(nc.sbuf_tensor("anout", [128, HC * TPC], BF16))
        g4a = open_group("P4A")
        antm = [g4a.enter_context(nc.sbuf_tensor(f"antm{i}", [128, H], BF16)) for i in range(2)]
        ps4a = open_group("PS4A")
        ps_t4 = [ps4a.enter_context(nc.psum_tensor(f"ps_t4{i}", [128, 128], BF16)) for i in range(2)]

        wo_ld = emit("gpsimd", [],
            lambda e: e.dma_start(wo_sb[:], wot[:]), "swo", 16)

        ac_ev4 = [[0] * 16 for _ in range(NT)]
        pe_tr4 = [0] * NT
        for tl in range(NT):
            b2 = tl % 2
            emit("vector", [],
                 lambda e, tl=tl: e.reciprocal(sm_t1[:, 0:1], sm_l[:, tl:tl + 1]), "sdv")
            w = []
            if tl >= 2:
                w.append(("spe", pe_tr4[tl - 2]))
            dv_an = emit("vector", w,
                 lambda e, b2=b2, tl=tl: e.tensor_scalar(
                     antm[b2][:], acc[:, tl * H:(tl + 1) * H], sm_t1[:, 0:1], None, ALU.mult), "sdv")
            for fc in range(16):
                slot = fc % 2
                w = [("sdv", dv_an)]
                if fc >= 2:
                    w.append(("sac", ac_ev4[tl][fc - 2]))
                elif tl >= 1:
                    w.append(("sac", ac_ev4[tl - 1][14 + fc]))
                pe_v = emit("tensor", w,
                    lambda e, b2=b2, fc=fc, slot=slot: e.transpose(
                        ps_t4[slot][:],
                        antm[b2][:, fc * 128:(fc + 1) * 128], idb[:]), "spe")
                ac_ev4[tl][fc] = emit("scalar", [("spe", pe_v)],
                    lambda e, fc=fc, slot=slot, tl=tl: e.copy(
                        anout[:, fc * TPC + tl * 128: fc * TPC + (tl + 1) * 128],
                        ps_t4[slot][:]), "sac")
            pe_tr4[tl] = S["spe"]

        barrier()
        close_group("PS4A")
        close_group("P4A")

        g4b = open_group("P4B")
        xt7 = [g4b.enter_context(nc.sbuf_tensor(f"xt7{i}", [128, H], F32)) for i in range(2)]
        sft = [g4b.enter_context(nc.sbuf_tensor(f"sft{i}", [128, HC * 128], BF16)) for i in range(2)]
        zt = [g4b.enter_context(nc.sbuf_tensor(f"zt{i}", [128, H], F32)) for i in range(2)]
        ps4b = open_group("PS4B")
        ps_o = ps4b.enter_context(nc.psum_tensor("ps_o", [128, H], F32))
        ps_t7 = ps4b.enter_context(nc.psum_tensor("ps_t7", [128, H], BF16))

        ld7 = [0] * NT
        pe_o = [0] * NT
        pe_t7 = [0] * NT
        dv_zo = [0] * NT
        dv_zt = [0] * NT
        st_out = [0] * NT
        for tl in range(NT):
            b2 = tl % 2
            w = []
            if tl >= 2:
                w.append(("sdv", dv_zo[tl - 2]))
            emit("sync", w,
                 lambda e, b2=b2, tl=tl: e.dma_start(xt7[b2][:], x_in[tl * 128:(tl + 1) * 128, :]),
                 f"sx7{b2}", 16)
            w = []
            if tl >= 2:
                w.append(("spe", pe_t7[tl - 2]))
            ld7[tl] = emit("sync", w,
                 lambda e, b2=b2, tl=tl: e.dma_start(
                     sft[b2].rearrange("p (fc k) -> p fc k", fc=HC),
                     s_ffn[:, tl * 128:(tl + 1) * 128]
                     .rearrange("(fc p) k -> p fc k", p=128)), f"sx7{b2}", 16)
            for hc in range(HC):
                for n in range(4):
                    w = []
                    if hc == 0 and n == 0:
                        w.append(("swo", wo_ld))
                        if tl >= 1:
                            w.append(("sdv", dv_zo[tl - 1]))
                    last = (hc == HC - 1 and n == 3)
                    pe_o[tl] = emit("tensor", w,
                        lambda e, hc=hc, n=n, tl=tl: e.matmul(
                            ps_o[:, n * 512:(n + 1) * 512],
                            anout[:, hc * TPC + tl * 128: hc * TPC + (tl + 1) * 128],
                            wo_sb[:, hc * H + n * 512: hc * H + (n + 1) * 512],
                            start=(hc == 0), stop=(hc == HC - 1)),
                        "spe" if last else None)
            for fc in range(16):
                w = []
                if fc == 0:
                    w.append((f"sx7{b2}", ld7[tl]))
                    if tl >= 1:
                        w.append(("sdv", dv_zt[tl - 1]))
                pe_t7[tl] = emit("tensor", w,
                    lambda e, b2=b2, fc=fc: e.transpose(
                        ps_t7[:, fc * 128:(fc + 1) * 128],
                        sft[b2][:, fc * 128:(fc + 1) * 128], idb[:]),
                    "spe" if fc == 15 else None)
            w = [(f"sx7{b2}", ld7[tl])]
            if tl >= 2:
                w.append((f"sout{b2}", st_out[tl - 2]))
            emit("vector", w,
                 lambda e, b2=b2, tl=tl: e.tensor_scalar(
                     zt[b2][:], xt7[b2][:], a_a[:, tl:tl + 1], b_a[:, tl:tl + 1], ALU.mult, ALU.add), "sdv")
            dv_zo[tl] = emit("vector", [("spe", pe_o[tl])],
                 lambda e, b2=b2: e.tensor_tensor(zt[b2][:], zt[b2][:], ps_o[:], ALU.add), "sdv")
            dv_zt[tl] = emit("vector", [("spe", pe_t7[tl])],
                 lambda e, b2=b2: e.tensor_tensor(zt[b2][:], zt[b2][:], ps_t7[:], ALU.add), "sdv")
            st_out[tl] = emit("sync", [("sdv", dv_zt[tl])],
                lambda e, b2=b2, tl=tl: e.dma_start(
                    out_d[tl * 128:(tl + 1) * 128, :], zt[b2][:]), f"sout{b2}", 16)

        final_w = [("sout0", S["sout0"]), ("sout1", S["sout1"])]
        emit("sync", final_w, lambda e: e.nop())

        with ExitStack() as sem_stack:
            for nm in S:
                semh[nm] = sem_stack.enter_context(nc.semaphore(nm))
            with nc.Block() as block:
                @block.tensor
                def _(eng):
                    for f in ops["tensor"]:
                        f(eng)

                @block.vector
                def _(eng):
                    for f in ops["vector"]:
                        f(eng)

                @block.scalar
                def _(eng):
                    for f in ops["scalar"]:
                        f(eng)

                @block.gpsimd
                def _(eng):
                    for f in ops["gpsimd"]:
                        f(eng)

                @block.sync
                def _(eng):
                    for f in ops["sync"]:
                        f(eng)
        close_group("PS4B")
        close_group("P4B")
        close_group("WO")
        close_group("ACC")
    return nc


_PERM = (np.arange(I_FF) % 2048) * 4 + (np.arange(I_FF) // 2048)


def _build_pmask(r):
    own = owned_tiles(r)
    NEG = np.float32(-1.0e30)
    tri = np.where(np.tril(np.ones((128, 128), dtype=bool)), np.float32(0), NEG)
    keep = np.zeros((128, 128), dtype=np.float32)
    drop = np.full((128, 128), NEG, dtype=np.float32)
    m = np.empty((NCH, NT, 128, 4, 128), dtype=np.float32)
    for c in range(NCH):
        for tl in range(NT):
            t = own[tl]
            for kbl in range(4):
                kb = 4 * c + kbl
                m[c, tl, :, kbl, :] = keep if kb < t else (tri if kb == t else drop)
    return np.ascontiguousarray(
        m.reshape(NCH * NT * 128, 512)).astype(ml_dtypes.bfloat16)


def kernel(L0_attn_x, L0_attn_qkv, L0_attn_o_proj, L0_attn_gamma, L0_attn_beta,
           L0_ffn_w_gate, L0_ffn_w_up, L0_ffn_w_down):
    x = np.asarray(L0_attn_x)
    qkv = np.asarray(L0_attn_qkv, dtype=np.float32)
    wo = np.asarray(L0_attn_o_proj, dtype=np.float32)
    gamma = np.asarray(L0_attn_gamma, dtype=np.float32)
    beta = np.asarray(L0_attn_beta, dtype=np.float32)
    wg = np.asarray(L0_ffn_w_gate, dtype=np.float32)[:, _PERM]
    wu = np.asarray(L0_ffn_w_up, dtype=np.float32)[:, _PERM]
    wd = np.asarray(L0_ffn_w_down, dtype=np.float32)[_PERM, :]

    qkt = np.ascontiguousarray(
        qkv[:, :2 * H].reshape(HC, 128, 32, 128).transpose(2, 1, 0, 3).reshape(32 * 128, HC * 128))
    wvt = np.ascontiguousarray(
        qkv[:, 2 * H:].reshape(HC, 128, 4, 512).transpose(2, 1, 0, 3).reshape(4 * 128, HC * 512))
    wot = np.ascontiguousarray(
        wo.reshape(HC, 128, H).transpose(1, 0, 2).reshape(128, HC * H))
    wgt = np.ascontiguousarray(
        wg.reshape(HC, 128, 64, 128).transpose(2, 1, 0, 3).reshape(64 * 128, HC * 128))
    wut = np.ascontiguousarray(
        wu.reshape(HC, 128, 64, 128).transpose(2, 1, 0, 3).reshape(64 * 128, HC * 128))
    wdt = np.ascontiguousarray(
        wd.reshape(64, 128, HC, 128).transpose(2, 1, 0, 3).reshape(HC * 128, 64 * 128))

    gam2 = np.ascontiguousarray(gamma.reshape(HC, 128).T)
    bet2 = np.ascontiguousarray(beta.reshape(HC, 128).T)
    idf = np.eye(128, dtype=np.float32)
    idb = np.eye(128, dtype=np.float32).astype(ml_dtypes.bfloat16)

    in_maps = []
    idxs = []
    for r in range(NCORES):
        idx = np.r_[512 * r:512 * (r + 1), S_TOT - 512 * (r + 1):S_TOT - 512 * r]
        idxs.append(idx)
        in_maps.append(dict(
            x=np.ascontiguousarray(x[0, idx, :], dtype=np.float32),
            qkt=qkt, wvt=wvt, wot=wot, wgt=wgt, wut=wut, wdt=wdt,
            gam2=gam2, bet2=bet2, idf=idf, idb=idb,
            pmask=_build_pmask(r)))

    nc = build_core()
    res = run_bass_kernel_spmd(nc, in_maps, core_ids=list(range(NCORES)))
    out = np.empty((1, S_TOT, H), dtype=np.float32)
    for r in range(NCORES):
        out[0, idxs[r], :] = res.results[r]["out"]
    return out


# version 19 (final)
# speedup vs baseline: 1.0832x; 1.0748x over previous
import math
from contextlib import ExitStack

import numpy as np
import ml_dtypes

import concourse.bass as bass
import concourse.mybir as mybir
from concourse.bass_utils import run_bass_kernel_spmd

F32 = mybir.dt.float32
BF16 = mybir.dt.bfloat16
AF = mybir.ActivationFunctionType
ALU = mybir.AluOpType
AX = mybir.AxisListType

NCORES = 8
S_TOT = 8192
H = 2048
I_FF = 8192
TPC = 1024
NT = 8
HC = 16
NCH = 16
SIG = 1.0 / math.sqrt(2048.0)
EPS = 1e-5


def owned_tiles(r):
    return [4 * r + i for i in range(4)] + [60 - 4 * r + i for i in range(4)]


def chunk_rank_half(c):
    return (c, 0) if c <= 7 else (15 - c, 1)


def build_core():
    nc = bass.Bass("TRN2", target_bir_lowering=False)

    x_in = nc.dram_tensor("x", [TPC, H], F32, kind="ExternalInput").ap()
    qkt = nc.dram_tensor("qkt", [32 * 128, HC * 128], F32, kind="ExternalInput").ap()
    wvt = nc.dram_tensor("wvt", [4 * 128, HC * 512], F32, kind="ExternalInput").ap()
    wot = nc.dram_tensor("wot", [128, HC * H], F32, kind="ExternalInput").ap()
    wgt = nc.dram_tensor("wgt", [64 * 128, HC * 128], F32, kind="ExternalInput").ap()
    wut = nc.dram_tensor("wut", [64 * 128, HC * 128], F32, kind="ExternalInput").ap()
    wdt = nc.dram_tensor("wdt", [16 * 128, 64 * 128], F32, kind="ExternalInput").ap()
    gam2 = nc.dram_tensor("gam2", [128, HC], F32, kind="ExternalInput").ap()
    bet2 = nc.dram_tensor("bet2", [128, HC], F32, kind="ExternalInput").ap()
    idf_d = nc.dram_tensor("idf", [128, 128], F32, kind="ExternalInput").ap()
    idb_d = nc.dram_tensor("idb", [128, 128], BF16, kind="ExternalInput").ap()
    pmask_d = nc.dram_tensor("pmask", [NCH * NT * 128, 512], BF16, kind="ExternalInput").ap()
    out_d = nc.dram_tensor("out", [TPC, H], F32, kind="ExternalOutput").ap()

    k_loc = nc.dram_tensor("k_loc", [H, TPC], BF16, kind="Internal").ap()
    v_loc = nc.dram_tensor("v_loc", [TPC, H], BF16, kind="Internal").ap()
    q_dram = nc.dram_tensor("q_dram", [H, TPC], BF16, kind="Internal").ap()
    k_gath = nc.dram_tensor("k_gath", [NCORES * H, TPC], BF16, kind="Internal").ap()
    v_gath = nc.dram_tensor("v_gath", [NCORES * TPC, H], BF16, kind="Internal").ap()
    s_ffn = nc.dram_tensor("s_ffn", [H, TPC], BF16, kind="Internal").ap()

    ENGS = ("tensor", "vector", "scalar", "gpsimd", "sync")
    ops = {E: [] for E in ENGS}
    S = {"scst": 0, "sx0": 0, "sx1": 0,
         "swq0": 0, "swq1": 0, "swv0": 0, "swv1": 0,
         "skst0": 0, "skst1": 0, "svst0": 0, "svst1": 0, "sqst0": 0, "sqst1": 0,
         "skv0": 0, "skv1": 0, "spm0": 0, "spm1": 0,
         "sql": 0, "swo": 0,
         "swg0": 0, "swg1": 0, "swu0": 0, "swu1": 0,
         "swd0": 0, "swd1": 0, "sfst0": 0, "sfst1": 0,
         "sx70": 0, "sx71": 0, "sout0": 0, "sout1": 0,
         "scc": 0, "spe": 0, "sdv": 0, "sac": 0, "spl": 0}
    semh = {}
    pend = {E: [] for E in ENGS}

    OWN = {"vector": "sdv", "scalar": "sac"}

    def emit(E, waits, fn, inc=None, n=1):
        w = list(pend[E]) + [(nm, v) for nm, v in waits if v > 0]
        if E in OWN and S[OWN[E]] > 0:
            w.append((OWN[E], S[OWN[E]]))
        pend[E].clear()
        if inc is not None:
            S[inc] += n
        def run(e, w=tuple(w), fn=fn, inc=inc, n=n):
            for nm, v in w:
                e.wait_ge(semh[nm], v)
            rr = fn(e)
            if inc is not None:
                rr.then_inc(semh[inc], n)
        ops[E].append(run)
        return S[inc] if inc is not None else None

    def barrier():
        snap = [(nm, v) for nm, v in S.items() if v > 0 and nm != "scc"]
        for E in ENGS:
            pend[E] = list(snap)

    stacks = {}

    def open_group(name):
        stacks[name] = ExitStack()
        return stacks[name]

    def close_group(name):
        stacks[name].close()
        del stacks[name]

    outer = ExitStack()
    with outer:
        e_ = outer.enter_context
        idf = e_(nc.sbuf_tensor("idf_s", [128, 128], F32))
        idb = e_(nc.sbuf_tensor("idb_s", [128, 128], BF16))
        gam = e_(nc.sbuf_tensor("gam_s", [128, HC], F32))
        bet = e_(nc.sbuf_tensor("bet_s", [128, HC], F32))
        epsc = e_(nc.sbuf_tensor("eps_s", [128, 1], F32))
        eps2c = e_(nc.sbuf_tensor("eps2_s", [128, 1], F32))
        mu_a = e_(nc.sbuf_tensor("mu_a", [128, NT], F32))
        rstd_a = e_(nc.sbuf_tensor("rstd_a", [128, NT], F32))
        a_a = e_(nc.sbuf_tensor("a_a", [128, NT], F32))
        b_a = e_(nc.sbuf_tensor("b_a", [128, NT], F32))
        sm_m = e_(nc.sbuf_tensor("sm_m", [128, 2 * NT], F32))
        sm_l = e_(nc.sbuf_tensor("sm_l", [128, NT], F32))
        sm_al = e_(nc.sbuf_tensor("sm_al", [128, NT], F32))
        sm_d = e_(nc.sbuf_tensor("sm_d", [128, NT], F32))
        sm_sc = e_(nc.sbuf_tensor("sm_sc", [128, NT], F32))
        sm_t1 = e_(nc.sbuf_tensor("sm_t1", [128, NT], F32))

        emit("sync", [], lambda e: e.dma_start(idf[:], idf_d[:]), "scst", 16)
        emit("sync", [], lambda e: e.dma_start(idb[:], idb_d[:]), "scst", 16)
        emit("sync", [], lambda e: e.dma_start(gam[:], gam2[:]), "scst", 16)
        emit("sync", [], lambda e: e.dma_start(bet[:], bet2[:]), "scst", 16)
        consts_done = S["scst"]
        emit("vector", [], lambda e: e.memset(epsc[:], EPS), "sdv")
        emit("vector", [], lambda e: e.memset(eps2c[:], EPS * EPS), "sdv")


        gXN = open_group("XN")
        xn_fm = gXN.enter_context(nc.sbuf_tensor("xn_fm", [128, HC * TPC], BF16))
        gA1 = open_group("A1")
        xnorm = gA1.enter_context(nc.sbuf_tensor("xnorm", [128, HC * TPC], BF16))
        g2 = open_group("P2")
        wqk = [g2.enter_context(nc.sbuf_tensor(f"wqk{i}", [128, HC * 128], BF16)) for i in range(2)]
        wvb = [g2.enter_context(nc.sbuf_tensor(f"wvb{i}", [128, HC * 512], BF16)) for i in range(2)]
        qkev = [g2.enter_context(nc.sbuf_tensor(f"qkev{i}", [128, TPC], BF16)) for i in range(2)]
        vev = [g2.enter_context(nc.sbuf_tensor(f"vev{i}", [128, 512], BF16)) for i in range(2)]
        ps2 = open_group("PS2")
        ps_qk = [ps2.enter_context(nc.psum_tensor(f"ps_qk{i}", [128, 1024], F32)) for i in range(2)]
        ps_v = [ps2.enter_context(nc.psum_tensor(f"ps_v{i}", [128, 512], F32)) for i in range(2)]
        g1 = open_group("P1")
        xt = [g1.enter_context(nc.sbuf_tensor(f"xt{i}", [128, H], F32)) for i in range(2)]
        xct = [g1.enter_context(nc.sbuf_tensor(f"xct{i}", [128, H], F32)) for i in range(2)]
        xnt = [g1.enter_context(nc.sbuf_tensor(f"xnt{i}", [128, H], F32)) for i in range(2)]
        yt = [g1.enter_context(nc.sbuf_tensor(f"yt{i}", [128, H], F32)) for i in range(2)]
        sqt = g1.enter_context(nc.sbuf_tensor("sqt", [128, H], F32))
        vr = g1.enter_context(nc.sbuf_tensor("vr", [128, 6], F32))
        R_a = g1.enter_context(nc.sbuf_tensor("R_a", [128, NT], F32))
        ps1 = open_group("PS1")
        ps_tr = [ps1.enter_context(nc.psum_tensor(f"ps_tr{i}", [128, 128], F32)) for i in range(2)]

        x_loaded = [0] * NT
        dv_tile = [0] * NT
        pe_trdone = [0] * NT
        ev_done = [[("sdv", 0)] * 32 for _ in range(NT)]
        dv_sub = [0] * NT
        dv_var = [0] * NT
        dv_xn = [0] * NT
        ac_sq_l = [0] * NT
        ac_exp2 = [0] * NT

        def stageA(tl):
            b2 = tl % 2
            w = [("scst", consts_done)]
            if tl >= 2:
                w.append(("sdv", dv_tile[tl - 2]))
            x_loaded[tl] = emit("sync", w,
                lambda e, tl=tl, b2=b2: e.dma_start(xt[b2][:], x_in[tl * 128:(tl + 1) * 128, :]),
                f"sx{b2}", 16)
            w = [(f"sx{b2}", x_loaded[tl])]
            if tl >= 2:
                w += [ev_done[tl - 2][31], ev_done[tl - 2][15]]
            emit("vector", w,
                 lambda e, b2=b2: e.reduce_sum(vr[:, 0:1], xt[b2][:], axis=AX.X), "sdv")
            emit("vector", [],
                 lambda e, tl=tl: e.tensor_scalar_mul(mu_a[:, tl:tl + 1], vr[:, 0:1], 1.0 / H), "sdv")
            dv_sub[tl] = emit("vector", [],
                 lambda e, b2=b2, tl=tl: e.tensor_scalar(xct[b2][:], xt[b2][:], mu_a[:, tl:tl + 1], None, ALU.subtract),
                 "sdv")

        def stageB(tl):
            w = [("sdv", dv_sub[tl])]
            if tl >= 1:
                w.append(("sdv", dv_var[tl - 1]))
            ac_sq_l[tl] = emit("scalar", w,
                 lambda e, b2=tl % 2: e.activation(sqt[:], xct[b2][:], AF.Square, accum_out=vr[:, 1:2]), "sac")

        def stageC(tl):
            emit("vector", [("sac", ac_sq_l[tl])],
                 lambda e: e.tensor_scalar_mul(vr[:, 4:5], vr[:, 1:2], (1.0 + EPS) / H), "sdv")
            dv_var[tl] = emit("vector", [],
                 lambda e: e.tensor_scalar_mul(vr[:, 1:2], vr[:, 1:2], 1.0 / H), "sdv")

        def stageL(tl):
            emit("scalar", [("sdv", dv_var[tl])],
                 lambda e: e.activation(vr[:, 2:3], vr[:, 1:2], AF.Ln, bias=epsc[:, 0:1]), "sac")
            emit("scalar", [],
                 lambda e, tl=tl: e.activation(rstd_a[:, tl:tl + 1], vr[:, 2:3], AF.Exp, scale=-0.5), "sac")
            emit("scalar", [],
                 lambda e: e.activation(vr[:, 5:6], vr[:, 4:5], AF.Ln, bias=eps2c[:, 0:1]), "sac")
            ac_exp2[tl] = emit("scalar", [],
                 lambda e, tl=tl: e.activation(R_a[:, tl:tl + 1], vr[:, 5:6], AF.Exp, scale=-0.5), "sac")

        def stageD(tl):
            b2 = tl % 2
            w = [("sac", ac_exp2[tl])]
            if tl >= 2:
                w.append(("spe", pe_trdone[tl - 2]))
            dv_xn[tl] = emit("vector", w,
                 lambda e, b2=b2, tl=tl: e.tensor_scalar(xnt[b2][:], xct[b2][:], rstd_a[:, tl:tl + 1], None, ALU.mult),
                 "sdv")
            dv_tile[tl] = emit("vector", [],
                 lambda e, b2=b2, tl=tl: e.tensor_scalar(yt[b2][:], xct[b2][:], R_a[:, tl:tl + 1], None, ALU.mult),
                 "sdv")

        def stageT(tl):
            b2 = tl % 2
            for idx in range(32):
                is_y = idx >= 16
                fc = idx % 16
                slot = idx % 2
                src = yt[b2] if is_y else xnt[b2]
                w = [("sdv", dv_tile[tl] if is_y else dv_xn[tl])]
                if idx >= 2:
                    w.append(ev_done[tl][idx - 2])
                elif tl >= 2:
                    w.append(ev_done[tl - 2][30 + idx])
                pe_v = emit("tensor", w,
                    lambda e, src=src, fc=fc, slot=slot:
                        e.transpose(ps_tr[slot][:], src[:, fc * 128:(fc + 1) * 128], idf[:]),
                    "spe")
                if is_y:
                    def ev(e, fc=fc, slot=slot, tl=tl):
                        return e.tensor_scalar(
                            xnorm[:, fc * TPC + tl * 128: fc * TPC + (tl + 1) * 128],
                            ps_tr[slot][:],
                            gam[:, fc:fc + 1], bet[:, fc:fc + 1], ALU.mult, ALU.add)
                    ev_done[tl][idx] = ("sdv", emit("vector", [("spe", pe_v)], ev, "sdv"))
                else:
                    def ev(e, fc=fc, slot=slot, tl=tl):
                        return e.copy(xn_fm[:, fc * TPC + tl * 128: fc * TPC + (tl + 1) * 128],
                                      ps_tr[slot][:])
                    ev_done[tl][idx] = ("sac", emit("scalar", [("spe", pe_v)], ev, "sac"))
            pe_trdone[tl] = S["spe"]

        stageA(0)
        stageB(0)
        stageA(1)
        for tl in range(NT):
            stageC(tl)
            stageL(tl)
            if tl + 1 < NT:
                stageB(tl + 1)
            stageD(tl)
            stageT(tl)
            if tl + 2 < NT:
                stageA(tl + 2)

        emit("vector", [("sac", S["sac"])],
             lambda e: e.tensor_scalar(a_a[:], rstd_a[:], 1.0, None, ALU.add), "sdv")
        emit("vector", [],
             lambda e: e.tensor_tensor(b_a[:], mu_a[:], rstd_a[:], ALU.mult), "sdv")
        emit("vector", [],
             lambda e: e.tensor_scalar_mul(b_a[:], b_a[:], -1.0), "sdv")

        close_group("PS1")
        close_group("P1")

        pe_vp = [0] * 32
        ac_vp = [0] * 32
        st_v = [0] * 32
        for cb in range(4):
            b2 = cb % 2
            w = []
            if cb >= 2:
                w.append(("spe", pe_vp[(cb - 2) * NT + NT - 1]))
            wldv = emit("gpsimd", w,
                lambda e, b2=b2, cb=cb: e.dma_start(
                    wvb[b2][:], wvt[cb * 128:(cb + 1) * 128, :]),
                f"swv{b2}", 16)
            for tl in range(NT):
                vi = cb * NT + tl
                pb = vi % 2
                for hc in range(HC):
                    w = []
                    if hc == 0:
                        w.append((f"swv{b2}", wldv))
                        w.append(ev_done[tl][31])
                        if vi >= 2:
                            w.append(("sac", ac_vp[vi - 2]))
                    pe_vp[vi] = emit("tensor", w,
                        lambda e, pb=pb, b2=b2, hc=hc, tl=tl: e.matmul(
                            ps_v[pb][:],
                            xnorm[:, hc * TPC + tl * 128: hc * TPC + (tl + 1) * 128],
                            wvb[b2][:, hc * 512:(hc + 1) * 512],
                            start=(hc == 0), stop=(hc == HC - 1)),
                        "spe" if hc == HC - 1 else None)
                w = [("spe", pe_vp[vi])]
                if vi >= 2:
                    w.append((f"svst{pb}", st_v[vi - 2]))
                ac_vp[vi] = emit("scalar", w,
                    lambda e, pb=pb: e.copy(vev[pb][:], ps_v[pb][:]), "sac")
                st_v[vi] = emit("sync", [("sac", ac_vp[vi])],
                    lambda e, pb=pb, tl=tl, cb=cb: e.dma_start(
                        v_loc[tl * 128:(tl + 1) * 128, cb * 512:(cb + 1) * 512], vev[pb][:]),
                    f"svst{pb}", 16)

        pe_qk = [0] * 32
        ac_qk = [0] * 32
        st_q = [0] * 16
        st_k = [0] * 16
        for m in range(32):
            is_q = m < 16
            fc = m % 16
            b2 = m % 2
            w = []
            if m >= 2:
                w.append(("spe", pe_qk[m - 2]))
            wld = emit("gpsimd", w,
                lambda e, b2=b2, m=m: e.dma_start(
                    wqk[b2][:], qkt[m * 128:(m + 1) * 128, :]),
                f"swq{b2}", 16)
            for hc in range(HC):
                for n in range(2):
                    w = []
                    if hc == 0 and n == 0:
                        w.append((f"swq{b2}", wld))
                        if m == 0:
                            w.append(ev_done[NT - 1][31])
                        if m >= 2:
                            w.append(("sac", ac_qk[m - 2]))
                    last = (hc == HC - 1 and n == 1)
                    pe_qk[m] = emit("tensor", w,
                        lambda e, b2=b2, hc=hc, n=n: e.matmul(
                            ps_qk[b2][:, n * 512:(n + 1) * 512],
                            wqk[b2][:, hc * 128:(hc + 1) * 128],
                            xnorm[:, hc * TPC + n * 512: hc * TPC + (n + 1) * 512],
                            start=(hc == 0), stop=(hc == HC - 1)),
                        "spe" if last else None)
            sem = f"sqst{b2}" if is_q else f"skst{b2}"
            starr = st_q if is_q else st_k
            w = [("spe", pe_qk[m])]
            if fc >= 2:
                w.append((sem, starr[fc - 2]))
            ac_qk[m] = emit("scalar", w,
                lambda e, b2=b2: e.copy(qkev[b2][:], ps_qk[b2][:]), "sac")
            dst = q_dram if is_q else k_loc
            starr[fc] = emit("sync", [("sac", ac_qk[m])],
                lambda e, b2=b2, fc=fc, dst=dst: e.dma_start(dst[fc * 128:(fc + 1) * 128, :], qkev[b2][:]),
                sem, 16)

        emit("gpsimd", [("svst0", S["svst0"]), ("svst1", S["svst1"]),
                        ("swq0", S["swq0"]), ("swq1", S["swq1"])],
            lambda e: e.collective_compute("AllGather", ALU.bypass,
                replica_groups=[list(range(NCORES))],
                ins=[v_loc.opt()], outs=[v_gath.opt()]), "scc", 1)
        emit("gpsimd", [("skst0", S["skst0"]), ("skst1", S["skst1"])],
            lambda e: e.collective_compute("AllGather", ALU.bypass,
                replica_groups=[list(range(NCORES))],
                ins=[k_loc.opt()], outs=[k_gath.opt()]), "scc", 1)

        barrier()
        close_group("PS2")
        close_group("P2")
        close_group("A1")

        g5o = open_group("HS")
        h_s = g5o.enter_context(nc.sbuf_tensor("h_s", [128, 64 * TPC], BF16))
        g5 = open_group("P5")
        wgb = [g5.enter_context(nc.sbuf_tensor(f"wgb{i}", [128, HC * 128], BF16)) for i in range(2)]
        wub = [g5.enter_context(nc.sbuf_tensor(f"wub{i}", [128, HC * 128], BF16)) for i in range(2)]
        gsb = [g5.enter_context(nc.sbuf_tensor(f"gsb{i}", [128, 512], F32)) for i in range(2)]
        hjt = [g5.enter_context(nc.sbuf_tensor(f"hjt{j}", [128, TPC], F32)) for j in range(4)]
        sjt = [g5.enter_context(nc.sbuf_tensor(f"sjt{j}", [128, 512], F32)) for j in range(4)]
        aux = [g5.enter_context(nc.sbuf_tensor(f"aux{j}", [128, 512], F32)) for j in range(1)]
        ps5 = open_group("PS5")
        ps_g = [ps5.enter_context(nc.psum_tensor(f"ps_g{i}", [128, 1024], F32)) for i in range(2)]
        ps_u = [ps5.enter_context(nc.psum_tensor(f"ps_u{i}", [128, 1024], F32)) for i in range(2)]

        pe_gu = {}
        dv_h = {}
        ac_silu = {}
        last_dvh = 0
        prev_h = {0: None, 1: None}
        for b in range(16):
            for j in range(4):
                ic = 16 * j + b
                q = b * 4 + j
                pp = q % 2
                w = []
                if q >= 2:
                    w.append(("spe", pe_gu[q - 2]))
                wldg = emit("gpsimd", w,
                    lambda e, pp=pp, ic=ic: e.dma_start(
                        wgb[pp][:], wgt[ic * 128:(ic + 1) * 128, :]),
                    f"swg{pp}", 16)
                wld = emit("gpsimd", [],
                    lambda e, pp=pp, ic=ic: e.dma_start(
                        wub[pp][:], wut[ic * 128:(ic + 1) * 128, :]),
                    f"swu{pp}", 16)
                for which in range(2):
                    wsrc = wgb[pp] if which == 0 else wub[pp]
                    pdst = ps_g[pp] if which == 0 else ps_u[pp]
                    for hc in range(HC):
                        for n in range(2):
                            w = []
                            if which == 0 and hc == 0 and n == 0:
                                w.append((f"swg{pp}", wldg))
                                w.append((f"swu{pp}", wld))
                                if q >= 2:
                                    w.append(("sdv", dv_h[(q - 2, 1)]))
                            last = (which == 1 and hc == HC - 1 and n == 1)
                            pe_gu[q] = emit("tensor", w,
                                lambda e, wsrc=wsrc, pdst=pdst, hc=hc, n=n: e.matmul(
                                    pdst[:, n * 512:(n + 1) * 512],
                                    wsrc[:, hc * 128:(hc + 1) * 128],
                                    xn_fm[:, hc * TPC + n * 512: hc * TPC + (n + 1) * 512],
                                    start=(hc == 0), stop=(hc == HC - 1)),
                                "spe" if last else None)
                for th in range(2):
                    w = [("spe", pe_gu[q])]
                    if prev_h[th] is not None:
                        w.append(("sdv", prev_h[th]))
                    ac_silu[(q, th)] = emit("scalar", w,
                        lambda e, pp=pp, th=th: e.activation(
                            gsb[th][:], ps_g[pp][:, th * 512:(th + 1) * 512], AF.Silu), "sac")
                    dv_h[(q, th)] = emit("vector", [("sac", ac_silu[(q, th)])],
                        lambda e, pp=pp, th=th, j=j: e.tensor_tensor(
                            hjt[j][:, th * 512:(th + 1) * 512], gsb[th][:],
                            ps_u[pp][:, th * 512:(th + 1) * 512], ALU.mult), "sdv")
                    prev_h[th] = dv_h[(q, th)]
                    last_dvh = dv_h[(q, th)]
                    if j == 3:
                        sl = slice(th * 512, (th + 1) * 512)
                        for jj in range(4):
                            emit("vector", [],
                                 lambda e, jj=jj, sl=sl: e.tensor_tensor(
                                     sjt[jj][:], hjt[jj][:, sl], hjt[jj][:, sl], ALU.mult), "sdv")
                        emit("vector", [], lambda e: e.tensor_tensor(aux[0][:], sjt[0][:], sjt[1][:], ALU.max), "sdv")
                        emit("vector", [], lambda e: e.tensor_tensor(sjt[0][:], sjt[0][:], sjt[1][:], ALU.min), "sdv")
                        emit("vector", [], lambda e: e.tensor_tensor(sjt[1][:], sjt[2][:], sjt[3][:], ALU.max), "sdv")
                        emit("vector", [], lambda e: e.tensor_tensor(sjt[2][:], sjt[2][:], sjt[3][:], ALU.min), "sdv")
                        emit("vector", [], lambda e: e.tensor_tensor(aux[0][:], aux[0][:], sjt[1][:], ALU.min), "sdv")
                        emit("vector", [], lambda e: e.tensor_tensor(sjt[0][:], sjt[0][:], sjt[2][:], ALU.max), "sdv")
                        emit("vector", [], lambda e: e.tensor_tensor(aux[0][:], aux[0][:], sjt[0][:], ALU.max), "sdv")
                        for jj in range(4):
                            ic2 = 16 * jj + b
                            emit("vector", [],
                                 lambda e, jj=jj, sl=sl: e.tensor_tensor(
                                     sjt[1][:], hjt[jj][:, sl], hjt[jj][:, sl], ALU.mult), "sdv")
                            emit("vector", [],
                                 lambda e: e.tensor_tensor(sjt[1][:], sjt[1][:], aux[0][:], ALU.is_ge), "sdv")
                            last_dvh = emit("vector", [],
                                 lambda e, jj=jj, sl=sl, ic2=ic2, th=th: e.tensor_tensor(
                                     h_s[:, ic2 * TPC + th * 512: ic2 * TPC + (th + 1) * 512],
                                     hjt[jj][:, sl], sjt[1][:], ALU.mult), "sdv")

        barrier()
        close_group("PS5")
        close_group("P5")

        g6 = open_group("P6")
        wdb = [g6.enter_context(nc.sbuf_tensor(f"wdb{i}", [128, 64 * 128], BF16)) for i in range(2)]
        sfev = [g6.enter_context(nc.sbuf_tensor(f"sfev{i}", [128, TPC], BF16)) for i in range(2)]
        ps6 = open_group("PS6")
        ps_f = [ps6.enter_context(nc.psum_tensor(f"ps_f{i}", [128, 1024], F32)) for i in range(2)]
        pe_f = [0] * 16
        ac_f = [0] * 16
        st_f = [0] * 16
        for fc in range(16):
            pb = fc % 2
            w = []
            if fc >= 2:
                w.append(("spe", pe_f[fc - 2]))
            wldd = emit("gpsimd", w,
                lambda e, pb=pb, fc=fc: e.dma_start(
                    wdb[pb][:], wdt[fc * 128:(fc + 1) * 128, :]),
                f"swd{pb}", 16)
            for ic in range(64):
                for n in range(2):
                    w = []
                    if ic == 0 and n == 0:
                        w.append((f"swd{pb}", wldd))
                        if fc >= 2:
                            w.append(("sac", ac_f[fc - 2]))
                    last = (ic == 63 and n == 1)
                    pe_f[fc] = emit("tensor", w,
                        lambda e, pb=pb, ic=ic, n=n: e.matmul(
                            ps_f[pb][:, n * 512:(n + 1) * 512],
                            wdb[pb][:, ic * 128:(ic + 1) * 128],
                            h_s[:, ic * TPC + n * 512: ic * TPC + (n + 1) * 512],
                            start=(ic == 0), stop=(ic == 63)),
                        "spe" if last else None)
            w = [("spe", pe_f[fc])]
            if fc >= 2:
                w.append((f"sfst{pb}", st_f[fc - 2]))
            ac_f[fc] = emit("scalar", w,
                lambda e, pb=pb: e.copy(sfev[pb][:], ps_f[pb][:]), "sac")
            st_f[fc] = emit("sync", [("sac", ac_f[fc])],
                lambda e, pb=pb, fc=fc: e.dma_start(
                    s_ffn[fc * 128:(fc + 1) * 128, :], sfev[pb][:]), f"sfst{pb}", 16)

        barrier()
        close_group("PS6")
        close_group("P6")
        close_group("HS")
        close_group("XN")

        gacc = open_group("ACC")
        acc = gacc.enter_context(nc.sbuf_tensor("acc", [128, NT * H], F32))
        g3 = open_group("P3")
        q_fm = g3.enter_context(nc.sbuf_tensor("q_fm", [128, HC * TPC], BF16))
        kbuf = [g3.enter_context(nc.sbuf_tensor(f"kbuf{i}", [128, HC * 512], BF16)) for i in range(2)]
        vbuf = [g3.enter_context(nc.sbuf_tensor(f"vbuf{i}", [128, 4 * H], BF16)) for i in range(2)]
        p_sb = [g3.enter_context(nc.sbuf_tensor(f"p_sb{i}", [128, 512], BF16)) for i in range(2)]
        pt_sb = [g3.enter_context(nc.sbuf_tensor(f"pt_sb{i}", [128, 512], BF16)) for i in range(2)]
        pm_sb = [g3.enter_context(nc.sbuf_tensor(f"pm_sb{i}", [128, NT * 512], BF16)) for i in range(2)]
        ps3 = open_group("PS3")
        ps_s = [ps3.enter_context(nc.psum_tensor(f"ps_s{i}", [128, 512], F32)) for i in range(2)]
        ps_pt = [ps3.enter_context(nc.psum_tensor(f"ps_pt{i}", [128, 512], BF16)) for i in range(2)]
        ps_pv = ps3.enter_context(nc.psum_tensor("ps_pv", [128, H], F32))

        q_ld = emit("sync", [],
            lambda e: e.dma_start(q_fm.rearrange("p (fc t) -> p fc t", fc=HC),
                                  q_dram.rearrange("(fc p) t -> p fc t", p=128)), "sql", 16)

        NP = NCH * NT
        kv_ld = [0] * NCH
        pm_ld = [0] * NCH
        pe_sc = [0] * NP
        pe_tr3 = [0] * NP
        pe_pv3 = [0] * NP
        dv_pmul = [0] * NP
        dv_rsum = [0] * NP
        dv_state = [0] * NP
        dv_ptev = [0] * NP
        dv_accu = [0] * NP
        dv_lup = [0] * NP
        ac_al = [0] * NP
        ac_ex = [0] * NP
        pe_chunk_end = [0] * NCH

        def emit_kv_load(c):
            rr, hh = chunk_rank_half(c)
            w = [("scc", 1)]
            if c >= 2:
                w.append(("spe", pe_chunk_end[c - 2]))
            emit("sync", w,
                lambda e, rr=rr, hh=hh, c=c: e.dma_start(
                    vbuf[c % 2].rearrange("p (kb f) -> p kb f", kb=4),
                    v_gath[rr * TPC + hh * 512: rr * TPC + (hh + 1) * 512, :]
                    .rearrange("(kb p) f -> p kb f", p=128)), f"skv{c % 2}", 16)
            kv_ld[c] = emit("sync", [("scc", 2)],
                lambda e, rr=rr, hh=hh, c=c: e.dma_start(
                    kbuf[c % 2].rearrange("p (fc k) -> p fc k", fc=HC),
                    k_gath[rr * H:(rr + 1) * H, hh * 512:(hh + 1) * 512]
                    .rearrange("(fc p) k -> p fc k", p=128)), f"skv{c % 2}", 16)

        def emit_pm_load(c):
            w = []
            if c >= 2:
                w.append(("spe", pe_chunk_end[c - 2]))
            pm_ld[c] = emit("scalar", w,
                lambda e, c=c: e.dma_start(
                    pm_sb[c % 2].rearrange("p (tl k) -> p tl k", tl=NT),
                    pmask_d[c * NT * 128:(c + 1) * NT * 128, :]
                    .rearrange("(tl p) k -> p tl k", p=128)),
                f"spm{c % 2}", 16)

        def emit_scores(i):
            c, tl = divmod(i, NT)
            sb = i % 2
            for fc in range(HC):
                w = []
                if fc == 0:
                    w.append((f"skv{c % 2}", kv_ld[c]))
                    w.append(("sql", q_ld))
                    if i >= 2:
                        w.append(("sac", ac_ex[i - 2]))
                emit("tensor", w,
                    lambda e, sb=sb, fc=fc, tl=tl, c=c: e.matmul(
                        ps_s[sb][:, :],
                        q_fm[:, fc * TPC + tl * 128: fc * TPC + (tl + 1) * 128],
                        kbuf[c % 2][:, fc * 512:(fc + 1) * 512],
                        start=(fc == 0), stop=False),
                    None)
            pe_sc[i] = emit("tensor", [(f"spm{c % 2}", pm_ld[c])],
                lambda e, sb=sb, c=c, tl=tl: e.matmul(
                    ps_s[sb][:, :], idb[:],
                    pm_sb[c % 2][:, tl * 512:(tl + 1) * 512],
                    start=False, stop=True),
                "spe")

        def emit_softmax(i):
            c, tl = divmod(i, NT)
            sb = i % 2
            first = (c == 0)
            mo = (c % 2) * NT + tl
            mp = ((c - 1) % 2) * NT + tl
            emit("vector", [("spe", pe_sc[i])],
                 lambda e, sb=sb: e.reduce_max(sm_t1[:, 0:1], ps_s[sb][:, :],
                                               axis=AX.X, negate=True), "sdv")
            if first:
                dv_state[i] = emit("vector", [],
                    lambda e, mo=mo: e.tensor_scalar_mul(sm_m[:, mo:mo + 1], sm_t1[:, 0:1], SIG), "sdv")
            else:
                dv_state[i] = emit("vector", [],
                    lambda e, mo=mo, mp=mp: e.tensor_scalar(
                        sm_m[:, mo:mo + 1], sm_t1[:, 0:1], SIG, sm_m[:, mp:mp + 1],
                        ALU.mult, ALU.min), "sdv")
            w = [("sdv", dv_state[i])]
            if i >= 2:
                w.append(("spe", pe_tr3[i - 2]))
            ac_ex[i] = emit("scalar", w,
                lambda e, sb=sb, pb=i % 2, mo=mo, tl=tl: e.activation(
                    p_sb[pb][:], ps_s[sb][:], AF.Exp,
                    bias=sm_m[:, mo:mo + 1], scale=SIG,
                    accum_out=sm_sc[:, tl:tl + 1]), "sac")
            dv_pmul[i] = 0
            dv_rsum[i] = 0

        def emit_alpha(i):
            c, tl = divmod(i, NT)
            if c == 0:
                return
            mo = (c % 2) * NT + tl
            mp = ((c - 1) % 2) * NT + tl
            dv_d = emit("vector", [],
                lambda e, mo=mo, mp=mp, tl=tl: e.tensor_tensor(
                    sm_d[:, tl:tl + 1], sm_m[:, mo:mo + 1], sm_m[:, mp:mp + 1], ALU.subtract), "sdv")
            ac_al[i] = emit("scalar", [("sdv", dv_d)],
                lambda e, tl=tl: e.activation(sm_al[:, tl:tl + 1], sm_d[:, tl:tl + 1], AF.Exp), "sac")

        def emit_tr(i):
            c, tl = divmod(i, NT)
            pb = i % 2
            for kb in range(4):
                w = []
                if kb == 0:
                    w.append(("sac", ac_ex[i]))
                    if i >= 2:
                        w.append(("sac", dv_ptev[i - 2]))
                pe_tr3[i] = emit("tensor", w,
                    lambda e, pb=pb, kb=kb: e.transpose(
                        ps_pt[pb][:, kb * 128:(kb + 1) * 128],
                        p_sb[pb][:, kb * 128:(kb + 1) * 128], idb[:]),
                    "spe" if kb == 3 else None)
            w = [("spe", pe_tr3[i])]
            if i >= 2:
                w.append(("spe", pe_pv3[i - 2]))
            dv_ptev[i] = emit("scalar", w,
                lambda e, pb=pb: e.copy(pt_sb[pb][:], ps_pt[pb][:]), "sac")

        def emit_pv(i):
            c, tl = divmod(i, NT)
            sb = i % 2
            first = (c == 0)
            for kb in range(4):
                for n in range(4):
                    w = []
                    if kb == 0 and n == 0:
                        w.append(("sac", dv_ptev[i]))
                        if i >= 1:
                            w.append(("sdv", dv_accu[i - 1]))
                    last = (kb == 3 and n == 3)
                    pe_pv3[i] = emit("tensor", w,
                        lambda e, sb=sb, kb=kb, n=n, c=c: e.matmul(
                            ps_pv[:, n * 512:(n + 1) * 512],
                            pt_sb[sb][:, kb * 128:(kb + 1) * 128],
                            vbuf[c % 2][:, kb * H + n * 512: kb * H + (n + 1) * 512],
                            start=(kb == 0), stop=(kb == 3)),
                        "spe" if last else None)
            w = [("spe", pe_pv3[i])]
            if not first:
                w.append(("sac", ac_al[i]))
            if first:
                dv_accu[i] = emit("vector", w,
                    lambda e, tl=tl: e.tensor_copy(acc[:, tl * H:(tl + 1) * H], ps_pv[:]), "sdv")
                dv_lup[i] = emit("vector", [("sac", ac_ex[i])],
                    lambda e, tl=tl: e.tensor_copy(sm_l[:, tl:tl + 1], sm_sc[:, tl:tl + 1]), "sdv")
            else:
                dv_accu[i] = emit("vector", w,
                    lambda e, tl=tl: e.scalar_tensor_tensor(
                        acc[:, tl * H:(tl + 1) * H], acc[:, tl * H:(tl + 1) * H],
                        sm_al[:, tl:tl + 1], ps_pv[:], ALU.mult, ALU.add), "sdv")
                dv_lup[i] = emit("vector", [("sac", ac_ex[i])],
                    lambda e, tl=tl: e.scalar_tensor_tensor(
                        sm_l[:, tl:tl + 1], sm_l[:, tl:tl + 1],
                        sm_al[:, tl:tl + 1], sm_sc[:, tl:tl + 1], ALU.mult, ALU.add), "sdv")

        emit_kv_load(0)
        emit_pm_load(0)
        emit_kv_load(1)
        emit_pm_load(1)
        emit_scores(0)
        emit_softmax(0)
        emit_scores(1)
        for i in range(NP):
            ci4, tl4 = divmod(i + 5, NT)
            if i + 5 < NP and tl4 == 0 and ci4 >= 2:
                pe_chunk_end[ci4 - 2] = S["spe"]
                emit_kv_load(ci4)
                emit_pm_load(ci4)
            emit_tr(i)
            if i + 1 < NP:
                emit_softmax(i + 1)
                emit_alpha(i + 1)
            if i + 2 < NP:
                emit_scores(i + 2)
            emit_pv(i)
        pe_chunk_end[NCH - 1] = S["spe"]

        barrier()
        close_group("PS3")
        close_group("P3")

        gWO = open_group("WO")
        wo_sb = gWO.enter_context(nc.sbuf_tensor("wo_sb", [128, HC * H], BF16))
        anout = gWO.enter_context(nc.sbuf_tensor("anout", [128, HC * TPC], BF16))
        g4a = open_group("P4A")
        antm = [g4a.enter_context(nc.sbuf_tensor(f"antm{i}", [128, H], BF16)) for i in range(2)]
        ps4a = open_group("PS4A")
        ps_t4 = [ps4a.enter_context(nc.psum_tensor(f"ps_t4{i}", [128, 128], BF16)) for i in range(2)]

        wo_ld = emit("gpsimd", [],
            lambda e: e.dma_start(wo_sb[:], wot[:]), "swo", 16)

        ac_ev4 = [[0] * 16 for _ in range(NT)]
        pe_tr4 = [0] * NT
        for tl in range(NT):
            b2 = tl % 2
            emit("vector", [],
                 lambda e, tl=tl: e.reciprocal(sm_t1[:, 0:1], sm_l[:, tl:tl + 1]), "sdv")
            w = []
            if tl >= 2:
                w.append(("spe", pe_tr4[tl - 2]))
            dv_an = emit("vector", w,
                 lambda e, b2=b2, tl=tl: e.tensor_scalar(
                     antm[b2][:], acc[:, tl * H:(tl + 1) * H], sm_t1[:, 0:1], None, ALU.mult), "sdv")
            for fc in range(16):
                slot = fc % 2
                w = [("sdv", dv_an)]
                if fc >= 2:
                    w.append(("sac", ac_ev4[tl][fc - 2]))
                elif tl >= 1:
                    w.append(("sac", ac_ev4[tl - 1][14 + fc]))
                pe_v = emit("tensor", w,
                    lambda e, b2=b2, fc=fc, slot=slot: e.transpose(
                        ps_t4[slot][:],
                        antm[b2][:, fc * 128:(fc + 1) * 128], idb[:]), "spe")
                ac_ev4[tl][fc] = emit("scalar", [("spe", pe_v)],
                    lambda e, fc=fc, slot=slot, tl=tl: e.copy(
                        anout[:, fc * TPC + tl * 128: fc * TPC + (tl + 1) * 128],
                        ps_t4[slot][:]), "sac")
            pe_tr4[tl] = S["spe"]

        barrier()
        close_group("PS4A")
        close_group("P4A")

        g4b = open_group("P4B")
        xt7 = [g4b.enter_context(nc.sbuf_tensor(f"xt7{i}", [128, H], F32)) for i in range(2)]
        sft = [g4b.enter_context(nc.sbuf_tensor(f"sft{i}", [128, HC * 128], BF16)) for i in range(2)]
        zt = [g4b.enter_context(nc.sbuf_tensor(f"zt{i}", [128, H], F32)) for i in range(2)]
        ps4b = open_group("PS4B")
        ps_o = ps4b.enter_context(nc.psum_tensor("ps_o", [128, H], F32))
        ps_t7 = ps4b.enter_context(nc.psum_tensor("ps_t7", [128, H], BF16))

        ld7 = [0] * NT
        pe_o = [0] * NT
        pe_t7 = [0] * NT
        dv_zo = [0] * NT
        dv_zt = [0] * NT
        st_out = [0] * NT
        for tl in range(NT):
            b2 = tl % 2
            w = []
            if tl >= 2:
                w.append(("sdv", dv_zo[tl - 2]))
            emit("sync", w,
                 lambda e, b2=b2, tl=tl: e.dma_start(xt7[b2][:], x_in[tl * 128:(tl + 1) * 128, :]),
                 f"sx7{b2}", 16)
            w = []
            if tl >= 2:
                w.append(("spe", pe_t7[tl - 2]))
            ld7[tl] = emit("sync", w,
                 lambda e, b2=b2, tl=tl: e.dma_start(
                     sft[b2].rearrange("p (fc k) -> p fc k", fc=HC),
                     s_ffn[:, tl * 128:(tl + 1) * 128]
                     .rearrange("(fc p) k -> p fc k", p=128)), f"sx7{b2}", 16)
            for hc in range(HC):
                for n in range(4):
                    w = []
                    if hc == 0 and n == 0:
                        w.append(("swo", wo_ld))
                        if tl >= 1:
                            w.append(("sdv", dv_zo[tl - 1]))
                    last = (hc == HC - 1 and n == 3)
                    pe_o[tl] = emit("tensor", w,
                        lambda e, hc=hc, n=n, tl=tl: e.matmul(
                            ps_o[:, n * 512:(n + 1) * 512],
                            anout[:, hc * TPC + tl * 128: hc * TPC + (tl + 1) * 128],
                            wo_sb[:, hc * H + n * 512: hc * H + (n + 1) * 512],
                            start=(hc == 0), stop=(hc == HC - 1)),
                        "spe" if last else None)
            for fc in range(16):
                w = []
                if fc == 0:
                    w.append((f"sx7{b2}", ld7[tl]))
                    if tl >= 1:
                        w.append(("sdv", dv_zt[tl - 1]))
                pe_t7[tl] = emit("tensor", w,
                    lambda e, b2=b2, fc=fc: e.transpose(
                        ps_t7[:, fc * 128:(fc + 1) * 128],
                        sft[b2][:, fc * 128:(fc + 1) * 128], idb[:]),
                    "spe" if fc == 15 else None)
            w = [(f"sx7{b2}", ld7[tl])]
            if tl >= 2:
                w.append((f"sout{b2}", st_out[tl - 2]))
            emit("vector", w,
                 lambda e, b2=b2, tl=tl: e.tensor_scalar(
                     zt[b2][:], xt7[b2][:], a_a[:, tl:tl + 1], b_a[:, tl:tl + 1], ALU.mult, ALU.add), "sdv")
            dv_zo[tl] = emit("vector", [("spe", pe_o[tl])],
                 lambda e, b2=b2: e.tensor_tensor(zt[b2][:], zt[b2][:], ps_o[:], ALU.add), "sdv")
            dv_zt[tl] = emit("vector", [("spe", pe_t7[tl])],
                 lambda e, b2=b2: e.tensor_tensor(zt[b2][:], zt[b2][:], ps_t7[:], ALU.add), "sdv")
            st_out[tl] = emit("sync", [("sdv", dv_zt[tl])],
                lambda e, b2=b2, tl=tl: e.dma_start(
                    out_d[tl * 128:(tl + 1) * 128, :], zt[b2][:]), f"sout{b2}", 16)

        final_w = [("sout0", S["sout0"]), ("sout1", S["sout1"])]
        emit("sync", final_w, lambda e: e.nop())

        with ExitStack() as sem_stack:
            for nm in S:
                semh[nm] = sem_stack.enter_context(nc.semaphore(nm))
            with nc.Block() as block:
                @block.tensor
                def _(eng):
                    for f in ops["tensor"]:
                        f(eng)

                @block.vector
                def _(eng):
                    for f in ops["vector"]:
                        f(eng)

                @block.scalar
                def _(eng):
                    for f in ops["scalar"]:
                        f(eng)

                @block.gpsimd
                def _(eng):
                    for f in ops["gpsimd"]:
                        f(eng)

                @block.sync
                def _(eng):
                    for f in ops["sync"]:
                        f(eng)
        close_group("PS4B")
        close_group("P4B")
        close_group("WO")
        close_group("ACC")
    return nc


_PERM = (np.arange(I_FF) % 2048) * 4 + (np.arange(I_FF) // 2048)


def _build_pmask(r):
    own = owned_tiles(r)
    NEG = np.float32(-1.0e30)
    tri = np.where(np.tril(np.ones((128, 128), dtype=bool)), np.float32(0), NEG)
    keep = np.zeros((128, 128), dtype=np.float32)
    drop = np.full((128, 128), NEG, dtype=np.float32)
    m = np.empty((NCH, NT, 128, 4, 128), dtype=np.float32)
    for c in range(NCH):
        for tl in range(NT):
            t = own[tl]
            for kbl in range(4):
                kb = 4 * c + kbl
                m[c, tl, :, kbl, :] = keep if kb < t else (tri if kb == t else drop)
    return np.ascontiguousarray(
        m.reshape(NCH * NT * 128, 512)).astype(ml_dtypes.bfloat16)


def kernel(L0_attn_x, L0_attn_qkv, L0_attn_o_proj, L0_attn_gamma, L0_attn_beta,
           L0_ffn_w_gate, L0_ffn_w_up, L0_ffn_w_down):
    x = np.asarray(L0_attn_x)
    qkv = np.asarray(L0_attn_qkv, dtype=np.float32)
    wo = np.asarray(L0_attn_o_proj, dtype=np.float32)
    gamma = np.asarray(L0_attn_gamma, dtype=np.float32)
    beta = np.asarray(L0_attn_beta, dtype=np.float32)
    wg = np.asarray(L0_ffn_w_gate, dtype=np.float32)[:, _PERM]
    wu = np.asarray(L0_ffn_w_up, dtype=np.float32)[:, _PERM]
    wd = np.asarray(L0_ffn_w_down, dtype=np.float32)[_PERM, :]

    qkt = np.ascontiguousarray(
        qkv[:, :2 * H].reshape(HC, 128, 32, 128).transpose(2, 1, 0, 3).reshape(32 * 128, HC * 128))
    wvt = np.ascontiguousarray(
        qkv[:, 2 * H:].reshape(HC, 128, 4, 512).transpose(2, 1, 0, 3).reshape(4 * 128, HC * 512))
    wot = np.ascontiguousarray(
        wo.reshape(HC, 128, H).transpose(1, 0, 2).reshape(128, HC * H))
    wgt = np.ascontiguousarray(
        wg.reshape(HC, 128, 64, 128).transpose(2, 1, 0, 3).reshape(64 * 128, HC * 128))
    wut = np.ascontiguousarray(
        wu.reshape(HC, 128, 64, 128).transpose(2, 1, 0, 3).reshape(64 * 128, HC * 128))
    wdt = np.ascontiguousarray(
        wd.reshape(64, 128, HC, 128).transpose(2, 1, 0, 3).reshape(HC * 128, 64 * 128))

    gam2 = np.ascontiguousarray(gamma.reshape(HC, 128).T)
    bet2 = np.ascontiguousarray(beta.reshape(HC, 128).T)
    idf = np.eye(128, dtype=np.float32)
    idb = np.eye(128, dtype=np.float32).astype(ml_dtypes.bfloat16)

    in_maps = []
    idxs = []
    for r in range(NCORES):
        idx = np.r_[512 * r:512 * (r + 1), S_TOT - 512 * (r + 1):S_TOT - 512 * r]
        idxs.append(idx)
        in_maps.append(dict(
            x=np.ascontiguousarray(x[0, idx, :], dtype=np.float32),
            qkt=qkt, wvt=wvt, wot=wot, wgt=wgt, wut=wut, wdt=wdt,
            gam2=gam2, bet2=bet2, idf=idf, idb=idb,
            pmask=_build_pmask(r)))

    nc = build_core()
    res = run_bass_kernel_spmd(nc, in_maps, core_ids=list(range(NCORES)))
    out = np.empty((1, S_TOT, H), dtype=np.float32)
    for r in range(NCORES):
        out[0, idxs[r], :] = res.results[r]["out"]
    return out
